# version 03
# baseline (speedup 1.0000x reference)
import contextlib
import numpy as np
import ml_dtypes

import concourse.bass as bass
import concourse.mybir as mybir
from concourse.bass_utils import run_bass_kernel_spmd
from concourse.alu_op_type import AluOpType as OP

F32 = mybir.dt.float32
BF16 = mybir.dt.bfloat16
AF = mybir.ActivationFunctionType
AX = mybir.AxisListType
BFDT = ml_dtypes.bfloat16

S = 4096
D = 2048
H = 16
ROPE = 64
LORA = 512
INTER = 8192
ST = 512
NST = S // ST
NKC = D // 128
NQC = 24
SCALE = float(np.float32(1.0 / np.sqrt(192.0)))
LN_EPS = 1e-5
RMS_EPS = 1.1920928955078125e-07
TP = 1024
NTP = S // TP

ENGS = ("te", "act", "dve", "pool", "sp")
QENG = {"sp": "sp", "scq": "act", "g": "pool"}


class Bk:
    __slots__ = ("t", "rd", "wr", "tok")

    def __init__(self, t):
        self.t = t
        self.rd = []
        self.wr = None
        self.tok = None

    def war(self):
        w = list(self.rd)
        self.rd = []
        return w


class Arena:

    def __init__(self, t, ncols):
        self.t = t
        self.n = ncols
        self.off = 0

    def reset(self):
        self.off = 0

    def alloc(self, p, cols):
        off = (self.off + 15) // 16 * 16
        assert off + cols <= self.n, f"arena overflow {off}+{cols}>{self.n}"
        self.off = off + cols
        return self.t[0:p, off:off + cols]


class Emitter:
    def __init__(self, nc, engine_name, handle, sems):
        self.nc = nc
        self.name = engine_name
        self.h = handle
        self.sems = sems
        self.c = {k: 0 for k in sems}
        self.sat = {}
        self.nbar = 0

    def _wait(self, tok):
        if tok is None:
            return
        sem, val = tok
        if self.sat.get(sem, 0) >= val:
            return
        self.h.wait_ge(self.sems[sem], val)
        self.sat[sem] = val

    def do(self, eng, fn, waits=(), inc=None, amt=1, fence=False):
        tok = None
        if inc is not None:
            self.c[inc] += amt
            tok = (inc, self.c[inc])
        if self.name == eng:
            for w in waits:
                self._wait(w)
            inst = fn(self.h)
            if inc is not None:
                if fence:
                    self.h.drain()
                    self.h.sem_inc(self.sems[inc], amt)
                else:
                    inst.then_inc(self.sems[inc], amt)
        return tok

    def dma(self, queue, out, in_, waits=()):
        return self.do(QENG[queue], lambda e: e.dma_start(out=out, in_=in_),
                       waits=waits, inc=queue, amt=16)

    def drain(self, eng):
        self.do(eng, lambda e: e.drain())

    def barrier(self):
        self.nbar += 1
        n = self.nbar * len(ENGS)
        for q, e in QENG.items():
            if e == self.name:
                self._wait((q, self.c[q]))
        if self.name != "sp":
            self.h.drain()
        self.h.sem_inc(self.sems["bar"], 1)
        self.c["bar"] = n
        self.h.wait_ge(self.sems["bar"], n)
        self.sat = {"bar": n}

    def mm(self, out, lhsT, rhs, start, stop, waits=(), inc=False):
        return self.do("te", lambda e: e.matmul(out, lhsT, rhs, start=start, stop=stop),
                       waits=waits, inc=("pe" if inc else None))

    def act(self, fn, waits=()):
        return self.do("act", fn, waits=waits, inc="act", fence=True)

    def dve(self, fn, waits=()):
        return self.do("dve", fn, waits=waits, inc="dve", fence=True)

    def pool(self, fn, waits=()):
        return self.do("pool", fn, waits=waits, inc="pool")


def build_program(phases="0ABC", dump=()):
    nc = bass.Bass("TRN2", target_bir_lowering=False)

    def dram(name, shape, dt, kind):
        if name in dump and kind == "Internal":
            kind = "ExternalOutput"
        return nc.dram_tensor(name, shape, dt, kind=kind).ap()

    x_in = dram("x", [S, D], F32, "ExternalInput")
    wq_in = dram("wq_p", [D, 3072], F32, "ExternalInput")
    wkva_in = dram("wkva_p", [D, 576], F32, "ExternalInput")
    wkvb_in = dram("wkvb_p", [LORA, 4096], F32, "ExternalInput")
    kvw_in = dram("kvw", [128, 4], F32, "ExternalInput")
    wo_in = dram("wo", [D, D], F32, "ExternalInput")
    wgu_in = dram("wguT", [D, 2 * INTER], F32, "ExternalInput")
    wdn_in = dram("wdownT", [INTER, D], F32, "ExternalInput")
    ident_in = dram("ident_bf", [128, 128], BF16, "ExternalInput")
    perm_in = dram("perm_bf", [128, 128], BF16, "ExternalInput")
    dup_in = dram("dup_bf", [64, 128], BF16, "ExternalInput")
    tri_in = dram("tri_bf", [128, 128], BF16, "ExternalInput")
    i128f_in = dram("i128_f32", [128, 128], F32, "ExternalInput")
    cos_in = dram("cosq", [128, S], F32, "ExternalInput")
    sin_in = dram("sinq", [128, S], F32, "ExternalInput")

    wqB = dram("wqB", [NQC, 128, D], BF16, "Internal")
    wguB = dram("wguB", [NKC, 128, 2 * INTER], BF16, "Internal")
    wdnB = dram("wdnB", [INTER, D], BF16, "Internal")
    woB = dram("woB", [D, D], BF16, "Internal")
    q_scr = dram("q_scr", [NQC, 128, S], BF16, "Internal")
    kn_scr = dram("kn_scr", [128, H, S], BF16, "Internal")
    v_scr = dram("v_scr", [S, D], BF16, "Internal")
    x1_scr = dram("x1_scr", [S, D], F32, "Internal")
    lrow_scr = dram("lrow_scr", [1, 8192], F32, "Internal")
    attn_scr = dram("attn_scr", [128, H, S], BF16, "Internal")
    p_dbg = dram("p_dbg", [128, 2048], BF16, "Internal")
    o_dbg = dram("o_dbg", [128, H * 512], F32, "Internal")
    ln2_dbg = dram("ln2_dbg", [NTP, 128, NKC * TP], BF16, "Internal")
    l_dbg = dram("l_dbg", [128, 64], F32, "Internal")
    linv_scr = dram("linv_scr", [NST, 64, 128], F32, "Internal")
    y_out = dram("y", [S, D], F32, "ExternalOutput")

    with contextlib.ExitStack() as top:
        E_ = top.enter_context
        ident = E_(nc.sbuf_tensor("ident", [128, 128], BF16))
        perm = E_(nc.sbuf_tensor("perm", [128, 128], BF16))
        dup = E_(nc.sbuf_tensor("dup", [64, 128], BF16))
        tri = E_(nc.sbuf_tensor("tri", [128, 128], BF16))
        i128f = E_(nc.sbuf_tensor("i128f", [128, 128], F32))
        ones_c = E_(nc.sbuf_tensor("ones_c", [128, 1], BF16))
        ones_rf = E_(nc.sbuf_tensor("ones_rf", [1, 128], F32))
        ones_rb = E_(nc.sbuf_tensor("ones_rb", [1, 128], BF16))
        one1 = E_(nc.sbuf_tensor("one1", [1, 1], BF16))
        kvw_sb = E_(nc.sbuf_tensor("kvw_sb", [128, 4], F32))
        nkmax = E_(nc.sbuf_tensor("nkmax", [1, 16], F32))
        krope2 = E_(nc.sbuf_tensor("krope2", [128, S], BF16))
        arena_bf_t = E_(nc.sbuf_tensor("arena_bf", [128, 63488], BF16))
        arena_f_t = E_(nc.sbuf_tensor("arena_f", [128, 18784], F32))
        const_sb = E_(nc.sbuf_tensor("const_sb", [128, 8], F32))
        CONSTS = [0.0, 1.0, LN_EPS, RMS_EPS, 1.0 / LORA, SCALE, -SCALE]
        pbt = [E_(nc.psum_tensor(f"pb{i}", [128, 512], F32)) for i in range(8)]
        sems = {k: E_(nc.semaphore(f"sem_{k}"))
                for k in ["sp", "scq", "g", "pe", "act", "dve", "pool", "bar"]}
        blk = E_(nc.Block())

        def schedule(E: Emitter):
            B = [Bk(t) for t in pbt]
            ab = Arena(arena_bf_t, 63488)
            af = Arena(arena_f_t, 18784)

            E.dma("g", ident[:], ident_in)
            E.dma("g", perm[:], perm_in)
            E.dma("g", dup[:], dup_in)
            E.dma("g", tri[:], tri_in)
            E.dma("g", i128f[:], i128f_in)
            E.dma("g", kvw_sb[:], kvw_in)
            E.pool(lambda e: e.memset(ones_c[:], 1.0))
            E.pool(lambda e: e.memset(ones_rf[:], 1.0))
            E.pool(lambda e: e.memset(ones_rb[:], 1.0))
            E.pool(lambda e: e.memset(one1[:], 1.0))
            E.pool(lambda e: e.memset(nkmax[:], 0.0))
            for ci, cv in enumerate(CONSTS):
                E.pool(lambda e, i2=ci, v=cv: e.memset(const_sb[:, i2:i2 + 1], v))
            if E.name == "sp":
                for ci, cv in enumerate(CONSTS):
                    nc.const_aps.register(F32, cv, const_sb[:, ci:ci + 1])
            E.barrier()

            if "0" in phases:
                if True:
                    ab.reset(); af.reset()
                    stq = [Bk(ab.alloc(128, 3072)) for k in range(2)]
                    stg = [Bk(ab.alloc(128, 8192)) for k in range(2)]
                    std = [Bk(ab.alloc(128, 2048)) for k in range(2)]
                    for kc in range(NKC):
                        s = stq[kc % 2]
                        tin = E.dma("g", s.t[:], wq_in[kc * 128:(kc + 1) * 128, :], waits=s.war())
                        for qc in range(NQC):
                            s.rd.append(E.dma("scq", wqB[qc, :, kc * 128:(kc + 1) * 128],
                                              s.t[:, qc * 128:(qc + 1) * 128], waits=[tin]))
                    for kc in range(NKC):
                        for hf in range(2):
                            s = stg[(kc * 2 + hf) % 2]
                            cols = slice(hf * 8192, (hf + 1) * 8192)
                            tin = E.dma("g", s.t[:], wgu_in[kc * 128:(kc + 1) * 128, cols],
                                        waits=s.war())
                            s.rd.append(E.dma("scq", wguB[kc, :, cols], s.t[:], waits=[tin]))
                    for r in range(INTER // 128):
                        s = std[r % 2]
                        tin = E.dma("g", s.t[:], wdn_in[r * 128:(r + 1) * 128, :], waits=s.war())
                        s.rd.append(E.dma("scq", wdnB[r * 128:(r + 1) * 128, :], s.t[:],
                                          waits=[tin]))
                    for r in range(D // 128):
                        s = std[r % 2]
                        tin = E.dma("g", s.t[:], wo_in[r * 128:(r + 1) * 128, :], waits=s.war())
                        s.rd.append(E.dma("scq", woB[r * 128:(r + 1) * 128, :], s.t[:],
                                          waits=[tin]))
                    E.barrier()

            if "A" in phases:
                if True:
                    ab.reset(); af.reset()
                    wkva = ab.alloc(128, NKC * 576)
                    wkvb_kn = ab.alloc(128, 4 * 2048)
                    wkvb_v = ab.alloc(128, 4 * 2048)
                    x_pt = [Bk(af.alloc(128, D)) for k in range(2)]
                    ln_tm = [Bk(ab.alloc(128, D)) for k in range(2)]
                    ln_fm = Bk(ab.alloc(128, NKC * 512))
                    wq_qc = [Bk(ab.alloc(128, D)) for k in range(2)]
                    q_st = [Bk(ab.alloc(128, 512)) for k in range(2)]
                    tmp_bf = [Bk(ab.alloc(128, 512)) for k in range(2)]
                    a_f32 = af.alloc(128, 512)
                    b_f32 = af.alloc(128, 512)
                    ckv_f = Bk(af.alloc(128, 4 * 512))
                    ckv_b = Bk(ab.alloc(128, 4 * 512))
                    sq_bf = [Bk(ab.alloc(128, 512)) for k in range(2)]
                    sq_kr = Bk(ab.alloc(64, 512))
                    kn_st = Bk(ab.alloc(128, H * 512))
                    v_st = [Bk(ab.alloc(128, 2048)) for k in range(2)]
                    cos_sl = [Bk(af.alloc(128, 512)) for k in range(2)]
                    sin_sl = [Bk(af.alloc(128, 512)) for k in range(2)]
                    rrow = [Bk(af.alloc(1, 512)) for k in range(4)]
                    rrbf = Bk(ab.alloc(1, 512))
                    nmx = Bk(af.alloc(1, 16))
                    lnst = af.alloc(128, 16)
                    xsq = Bk(af.alloc(128, D))

                    tkva = None
                    for kc in range(NKC):
                        tkva = E.dma("g", wkva[:, kc * 576:(kc + 1) * 576],
                                     wkva_in[kc * 128:(kc + 1) * 128, :])
                    tkvb = None
                    for lc in range(4):
                        tkvb = E.dma("g", wkvb_kn[:, lc * 2048:(lc + 1) * 2048],
                                     wkvb_in[lc * 128:(lc + 1) * 128, 0:2048])
                        tkvb = E.dma("g", wkvb_v[:, lc * 2048:(lc + 1) * 2048],
                                     wkvb_in[lc * 128:(lc + 1) * 128, 2048:4096])
                    tfold = None
                    for lc in range(4):
                        for buf in (wkvb_kn, wkvb_v):
                            tfold = E.dve(lambda e, b=buf, l=lc: e.tensor_scalar(
                                out=b[:, l * 2048:(l + 1) * 2048],
                                in0=b[:, l * 2048:(l + 1) * 2048],
                                scalar1=kvw_sb[:, l:l + 1], scalar2=None, op0=OP.mult),
                                waits=[tkvb])
                    E.drain("dve")

                    prev = {}
                    for i in range(NST):
                        tok0 = i * ST
                        cs = cos_sl[i % 2]
                        sn = sin_sl[i % 2]
                        tcos = E.dma("sp", cs.t[:], cos_in[:, tok0:tok0 + 512], waits=cs.war())
                        tsin = E.dma("sp", sn.t[:], sin_in[:, tok0:tok0 + 512], waits=sn.war())

                        for pt in range(4):
                            xs = x_pt[pt % 2]
                            tx = E.dma("sp", xs.t[:],
                                       x_in[tok0 + pt * 128:tok0 + (pt + 1) * 128, :],
                                       waits=xs.war())
                            t1 = E.dve(lambda e, x=xs: e.reduce_sum(out=lnst[:, 0:1], in_=x.t[:],
                                                                    axis=AX.X), waits=[tx])
                            t2 = E.act(lambda e, x=xs: e.activation(out=xsq.t[:], in_=x.t[:],
                                                                    func=AF.Square),
                                       waits=xsq.war() + [tx])
                            xs.rd += [t1, t2]
                            t3 = E.dve(lambda e: e.reduce_sum(out=lnst[:, 1:2], in_=xsq.t[:],
                                                              axis=AX.X), waits=[t2])
                            xsq.rd.append(t3)
                            E.drain("dve")
                            t4 = E.dve(lambda e: e.tensor_scalar(
                                out=lnst[:, 2:3], in0=lnst[:, 0:1], scalar1=-1.0 / D,
                                scalar2=None, op0=OP.mult), waits=[prev.get("ln5")])
                            E.drain("dve")
                            t5 = E.act(lambda e: e.activation(out=lnst[:, 3:4], in_=lnst[:, 2:3],
                                                              func=AF.Square),
                                       waits=[t4, prev.get("ln6")])
                            prev["ln5"] = t5
                            t6 = E.dve(lambda e: e.scalar_tensor_tensor(
                                out=lnst[:, 4:5], in0=lnst[:, 1:2], scalar=1.0 / D,
                                in1=lnst[:, 3:4], op0=OP.mult, op1=OP.subtract),
                                waits=[t3, t5])
                            prev["ln6"] = t6
                            t7 = E.act(lambda e: e.activation(out=lnst[:, 5:6], in_=lnst[:, 4:5],
                                                              func=AF.Sqrt, bias=LN_EPS,
                                                              scale=1.0),
                                       waits=[t6, prev.get("ln8")])
                            t8 = E.dve(lambda e: e.reciprocal(out=lnst[:, 6:7], in_=lnst[:, 5:6]),
                                       waits=[t7])
                            prev["ln8"] = t8
                            E.drain("dve")
                            ls = ln_tm[pt % 2]
                            t9 = E.dve(lambda e, l=ls, x=xs: e.tensor_scalar(
                                out=l.t[:], in0=x.t[:], scalar1=lnst[:, 2:3],
                                scalar2=lnst[:, 6:7], op0=OP.add, op1=OP.mult),
                                waits=ls.war() + [t8])
                            xs.rd.append(t9)
                            E.drain("dve")
                            for kcg in range(4):
                                bk = B[kcg % 2]
                                w0 = bk.war() + ln_fm.war() + [t9]
                                tg = None
                                for j in range(4):
                                    kc = kcg * 4 + j
                                    tg = E.mm(bk.t[:, j * 128:(j + 1) * 128],
                                              ls.t[:, kc * 128:(kc + 1) * 128], ident[:],
                                              start=True, stop=True,
                                              waits=w0 if j == 0 else (), inc=(j == 3))
                                bk.wr = tg
                                ls.rd.append(tg)
                                out_v = ln_fm.t[:, :].rearrange("p (kc t) -> p kc t", t=512)[
                                    :, kcg * 4:(kcg + 1) * 4, pt * 128:(pt + 1) * 128]
                                te = E.act(lambda e, b=bk, o=out_v: e.activation(
                                    out=o, in_=b.t[:, :].rearrange("p (j m) -> p j m", m=128),
                                    func=AF.Copy), waits=[tg])
                                bk.rd.append(te)
                                ln_fm.wr = te
                        tok_lnfm = ("act", E.c["act"])

                        for qc in range(NQC):
                            ws = wq_qc[qc % 2]
                            tw = E.dma("sp", ws.t[:], wqB[qc, :, :], waits=ws.war())
                            bk = B[2 + qc % 2]
                            w0 = bk.war() + [tw, tok_lnfm]
                            tg = None
                            for kc in range(NKC):
                                tg = E.mm(bk.t[:], ws.t[:, kc * 128:(kc + 1) * 128],
                                          ln_fm.t[:, kc * 512:(kc + 1) * 512],
                                          start=(kc == 0), stop=(kc == NKC - 1),
                                          waits=w0 if kc == 0 else (), inc=(kc == NKC - 1))
                            bk.wr = tg
                            ws.rd.append(tg)
                            ln_fm.rd.append(tg)
                            qs = q_st[qc % 2]
                            if qc < 16:
                                tq = E.act(lambda e, b=bk, q=qs: e.activation(
                                    out=q.t[:], in_=b.t[:], func=AF.Copy),
                                    waits=qs.war() + [tg])
                                bk.rd.append(tq)
                            else:
                                ts = tmp_bf[qc % 2]
                                tev = E.act(lambda e, b=bk, t=ts: e.activation(
                                    out=t.t[:], in_=b.t[:], func=AF.Copy),
                                    waits=ts.war() + [tg])
                                bk.rd.append(tev)
                                bp = B[4 + qc % 2]
                                wp = bp.war() + [tev]
                                tp = E.mm(bp.t[:], perm[:], ts.t[:], start=True, stop=True,
                                          waits=wp, inc=True)
                                bp.wr = tp
                                ts.rd.append(tp)
                                ta = E.dve(lambda e, t=ts, c=cs: e.tensor_tensor(
                                    out=a_f32[:], in0=t.t[:], in1=c.t[:], op=OP.mult),
                                    waits=[tev, tcos])
                                ts.rd.append(ta)
                                cs.rd.append(ta)
                                tb = E.dve(lambda e, b=bp, s=sn: e.tensor_tensor(
                                    out=b_f32[:], in0=b.t[:], in1=s.t[:], op=OP.mult),
                                    waits=[tp, tsin])
                                bp.rd.append(tb)
                                sn.rd.append(tb)
                                E.drain("dve")
                                tq = E.dve(lambda e, q=qs: e.tensor_tensor(
                                    out=q.t[:], in0=a_f32[:], in1=b_f32[:], op=OP.add),
                                    waits=qs.war())
                                E.drain("dve")
                            qs.rd.append(E.dma("scq", q_scr[qc, :, tok0:tok0 + 512], qs.t[:],
                                               waits=[tq]))

                        br = B[6]
                        wbr = br.war()
                        tgr = None
                        for oc in range(4):
                            bk = B[2 + oc % 2]
                            w0 = bk.war() + [tok_lnfm, tkva]
                            tg = None
                            for kc in range(NKC):
                                tg = E.mm(bk.t[:],
                                          wkva[:, kc * 576 + oc * 128:kc * 576 + (oc + 1) * 128],
                                          ln_fm.t[:, kc * 512:(kc + 1) * 512],
                                          start=(kc == 0), stop=(kc == NKC - 1),
                                          waits=w0 if kc == 0 else (), inc=(kc == NKC - 1))
                            bk.wr = tg
                            ln_fm.rd.append(tg)
                            tev = E.act(lambda e, b=bk, o=oc: e.activation(
                                out=ckv_f.t[:, o * 512:(o + 1) * 512], in_=b.t[:],
                                func=AF.Copy), waits=(ckv_f.war() if oc == 0 else []) + [tg])
                            bk.rd.append(tev)
                            ckv_f.wr = tev
                            sq = sq_bf[oc % 2]
                            tsq = E.act(lambda e, b=bk, s=sq: e.activation(
                                out=s.t[:], in_=b.t[:], func=AF.Square),
                                waits=sq.war() + [tg])
                            bk.rd.append(tsq)
                            tgr = E.mm(br.t[0:1, :], ones_c[:], sq.t[:],
                                       start=(oc == 0), stop=(oc == 3),
                                       waits=(wbr if oc == 0 else []) + [tsq], inc=(oc == 3))
                            sq.rd.append(tgr)
                        br.wr = tgr
                        r0 = rrow[0]
                        tsd = E.act(lambda e, r=r0: e.activation(
                            out=r.t[:], in_=br.t[0:1, :], func=AF.Sqrt,
                            bias=RMS_EPS, scale=1.0 / LORA), waits=r0.war() + [tgr])
                        br.rd.append(tsd)
                        r1 = rrow[1]
                        trr = E.dve(lambda e, a=r0, b=r1: e.reciprocal(out=b.t[:], in_=a.t[:]),
                                    waits=r1.war() + [tsd])
                        r0.rd.append(trr)
                        trb = E.act(lambda e, b=r1: e.activation(out=rrbf.t[:], in_=b.t[:],
                                                                 func=AF.Copy),
                                    waits=rrbf.war() + [trr])
                        r1.rd.append(trb)
                        bb = B[5]
                        tbc = E.mm(bb.t[:], ones_rb[:], rrbf.t[:], start=True, stop=True,
                                   waits=bb.war() + [trb], inc=True)
                        bb.wr = tbc
                        rrbf.rd.append(tbc)
                        tok_ckv = None
                        for oc in range(4):
                            tn = E.dve(lambda e, o=oc, b=bb: e.tensor_tensor(
                                out=ckv_b.t[:, o * 512:(o + 1) * 512],
                                in0=ckv_f.t[:, o * 512:(o + 1) * 512],
                                in1=b.t[:], op=OP.mult),
                                waits=(ckv_b.war() if oc == 0 else []) + [tbc, ckv_f.wr])
                            bb.rd.append(tn)
                            ckv_f.rd.append(tn)
                            tok_ckv = tn
                        bk = B[3]
                        w0 = bk.war() + [tok_lnfm]
                        tg = None
                        for kc in range(NKC):
                            tg = E.mm(bk.t[0:64, :],
                                      wkva[:, kc * 576 + 512:kc * 576 + 576],
                                      ln_fm.t[:, kc * 512:(kc + 1) * 512],
                                      start=(kc == 0), stop=(kc == NKC - 1),
                                      waits=w0 if kc == 0 else (), inc=(kc == NKC - 1))
                        bk.wr = tg
                        ln_fm.rd.append(tg)
                        ts = tmp_bf[0]
                        tev = E.act(lambda e, t=ts: e.activation(out=t.t[0:64, :],
                                                                 in_=bk.t[0:64, :],
                                                                 func=AF.Copy),
                                    waits=ts.war() + [tg])
                        bk.rd.append(tev)
                        bp = B[4]
                        tp = E.mm(bp.t[0:64, :], perm[0:64, 0:64], ts.t[0:64, :],
                                  start=True, stop=True, waits=bp.war() + [tev], inc=True)
                        bp.wr = tp
                        ts.rd.append(tp)
                        ta = E.dve(lambda e, t=ts, c=cs: e.tensor_tensor(
                            out=a_f32[0:64, :], in0=t.t[0:64, :], in1=c.t[0:64, :],
                            op=OP.mult), waits=[tev, tcos])
                        ts.rd.append(ta)
                        cs.rd.append(ta)
                        tb = E.dve(lambda e, b=bp, s=sn: e.tensor_tensor(
                            out=b_f32[0:64, :], in0=b.t[0:64, :], in1=s.t[0:64, :],
                            op=OP.mult), waits=[tp, tsin])
                        bp.rd.append(tb)
                        sn.rd.append(tb)
                        E.drain("dve")
                        trot = E.dve(lambda e, t=ts: e.tensor_tensor(
                            out=t.t[0:64, :], in0=a_f32[0:64, :], in1=b_f32[0:64, :],
                            op=OP.add), waits=[ta, tb])
                        ts.rd.append(trot)
                        E.drain("dve")
                        bp2 = B[5]
                        tdup = E.mm(bp2.t[:], dup[:], ts.t[0:64, :], start=True, stop=True,
                                    waits=bp2.war() + [trot], inc=True)
                        bp2.wr = tdup
                        ts.rd.append(tdup)
                        tk2 = E.act(lambda e: e.activation(out=krope2[:, tok0:tok0 + 512],
                                                           in_=bp2.t[:], func=AF.Copy),
                                    waits=[tdup])
                        bp2.rd.append(tk2)
                        tskr = E.act(lambda e: e.activation(out=sq_kr.t[:], in_=bp2.t[0:64, :],
                                                            func=AF.Square),
                                     waits=sq_kr.war() + [tdup])
                        bp2.rd.append(tskr)

                        kn_toks = []
                        for h in range(H):
                            bk = B[2 + h % 2]
                            w0 = bk.war() + [tok_ckv, tfold]
                            tg = None
                            for lc in range(4):
                                tg = E.mm(bk.t[:],
                                          wkvb_kn[:, lc * 2048 + h * 128:lc * 2048 + (h + 1) * 128],
                                          ckv_b.t[:, lc * 512:(lc + 1) * 512],
                                          start=(lc == 0), stop=(lc == 3),
                                          waits=w0 if lc == 0 else (), inc=(lc == 3))
                            bk.wr = tg
                            ckv_b.rd.append(tg)
                            tev = E.act(lambda e, b=bk, hh=h: e.activation(
                                out=kn_st.t[:, hh * 512:(hh + 1) * 512], in_=b.t[:],
                                func=AF.Copy), waits=(kn_st.war() if h == 0 else []) + [tg])
                            bk.rd.append(tev)
                            kn_toks.append(tev)
                            sq = sq_bf[h % 2]
                            tsq = E.act(lambda e, b=bk, s=sq: e.activation(
                                out=s.t[:], in_=b.t[:], func=AF.Square),
                                waits=sq.war() + [tg])
                            bk.rd.append(tsq)
                            bn = B[6 + h % 2]
                            wn = bn.war() + [tsq]
                            E.mm(bn.t[0:1, :], ones_c[:], sq.t[:], start=True, stop=False,
                                 waits=wn)
                            tnk = E.mm(bn.t[0:1, :], ones_c[0:64, :], sq_kr.t[:],
                                       start=False, stop=True, waits=[tskr], inc=True)
                            bn.wr = tnk
                            sq.rd.append(tnk)
                            sq_kr.rd.append(tnk)
                            rr = rrow[2 + h % 2]
                            tevn = E.act(lambda e, b=bn, r=rr: e.activation(
                                out=r.t[:], in_=b.t[0:1, :], func=AF.Copy),
                                waits=rr.war() + [tnk])
                            bn.rd.append(tevn)
                            tred = E.dve(lambda e, r=rr, n=nmx: e.reduce_max(
                                out=n.t[0:1, h % 2:h % 2 + 1], in_=r.t[:], axis=AX.X),
                                waits=nmx.war() + [tevn])
                            rr.rd.append(tred)
                            E.drain("dve")
                            tmx = E.dve(lambda e, hh=h, n=nmx: e.tensor_tensor(
                                out=nkmax[0:1, hh:hh + 1], in0=nkmax[0:1, hh:hh + 1],
                                in1=n.t[0:1, hh % 2:hh % 2 + 1], op=OP.max), waits=[tred])
                            nmx.rd.append(tmx)
                            E.drain("dve")
                        kn_st.rd.append(
                            E.dma("scq", kn_scr[:, :, tok0:tok0 + 512],
                                  kn_st.t[:, :].rearrange("p (h t) -> p h t", t=512),
                                  waits=kn_toks))

                        for pt in range(4):
                            vs = v_st[pt % 2]
                            v_toks = []
                            for vg in range(4):
                                bk = B[2 + vg % 2]
                                w0 = bk.war() + [tok_ckv, tfold]
                                tg = None
                                for lc in range(4):
                                    tg = E.mm(bk.t[:],
                                              ckv_b.t[:, lc * 512 + pt * 128:lc * 512 + (pt + 1) * 128],
                                              wkvb_v[:, lc * 2048 + vg * 512:lc * 2048 + (vg + 1) * 512],
                                              start=(lc == 0), stop=(lc == 3),
                                              waits=w0 if lc == 0 else (), inc=(lc == 3))
                                bk.wr = tg
                                ckv_b.rd.append(tg)
                                tev = E.act(lambda e, b=bk, v=vg, s=vs: e.activation(
                                    out=s.t[:, v * 512:(v + 1) * 512],
                                    in_=b.t[:], func=AF.Copy),
                                    waits=(vs.war() if vg == 0 else []) + [tg])
                                bk.rd.append(tev)
                                v_toks.append(tev)
                            vs.rd.append(
                                E.dma("scq", v_scr[tok0 + pt * 128:tok0 + (pt + 1) * 128, :],
                                      vs.t[:], waits=v_toks))
                    E.barrier()

            if "B" in phases:
                if True:
                    ab.reset(); af.reset()
                    q_sb = Bk(ab.alloc(128, NQC * 512))
                    knb = [Bk(ab.alloc(128, 8 * 512)) for k in range(3)]
                    vbuf = [Bk(ab.alloc(128, 2 * 2048)) for k in range(3)]
                    O_sb = Bk(af.alloc(128, H * 512))
                    O_bf = Bk(ab.alloc(128, H * 512))
                    P_sb = Bk(ab.alloc(128, 4 * 512))
                    PT_sb = [Bk(ab.alloc(128, 512)) for k in range(4)]
                    bias_sb = Bk(af.alloc(128, 64))
                    l_col = Bk(af.alloc(128, 64))
                    l_tmp = [Bk(af.alloc(128, 1)) for k in range(2)]
                    l_tmp4 = Bk(af.alloc(128, 4))
                    sq_m = [Bk(ab.alloc(128, 512)) for k in range(2)]
                    mrow = Bk(af.alloc(1, 512))
                    mrow_bf = Bk(ab.alloc(1, 512))
                    lrows = [Bk(af.alloc(1, 512)) for k in range(2)]
                    l_bf = Bk(ab.alloc(128, 64))
                    rrow_b = Bk(af.alloc(1, 512))
                    rrow_bb = Bk(ab.alloc(1, 512))
                    lT = Bk(af.alloc(64, 128))
                    wo_dc = Bk(ab.alloc(128, H * 512))
                    x_jit = Bk(af.alloc(128, 4 * 512))
                    x1_st = [Bk(af.alloc(128, 512)) for k in range(2)]
                    lrow_state = Bk(None)

                    for i in range(NST):
                        tok0 = i * ST
                        tq = E.dma("sp", q_sb.t[:],
                                   q_scr[:, :, tok0:tok0 + 512].rearrange("qc p t -> p qc t"),
                                   waits=q_sb.war())
                        bbs = B[6]
                        wbias = bbs.war()
                        first_bias = True
                        tt = None
                        for hp in range(8):
                            sqr = sq_m[0]
                            trp = E.act(lambda e, c=16 + hp, s=sqr: e.activation(
                                out=s.t[:], in_=q_sb.t[:, c * 512:(c + 1) * 512],
                                func=AF.Square), waits=sqr.war() + [tq])
                            q_sb.rd.append(trp)
                            for hh in range(2):
                                h = 2 * hp + hh
                                sqn = sq_m[1]
                                tn = E.act(lambda e, c=h, s=sqn: e.activation(
                                    out=s.t[:], in_=q_sb.t[:, c * 512:(c + 1) * 512],
                                    func=AF.Square), waits=sqn.war() + [tq])
                                q_sb.rd.append(tn)
                                bq = B[4 + h % 2]
                                wn = bq.war() + [tn]
                                E.mm(bq.t[0:1, :], ones_c[:], sqn.t[:], start=True, stop=False,
                                     waits=wn)
                                rof = hh * 64
                                tnq = E.mm(bq.t[0:1, :], ones_c[rof:rof + 64, :],
                                           sqr.t[rof:rof + 64, :], start=False, stop=True,
                                           waits=[trp], inc=True)
                                bq.wr = tnq
                                sqn.rd.append(tnq)
                                sqr.rd.append(tnq)
                                tm1 = E.dve(lambda e, hh2=h, b=bq: e.tensor_scalar(
                                    out=mrow.t[:], in0=b.t[0:1, :],
                                    scalar1=nkmax[0:1, hh2:hh2 + 1], scalar2=None,
                                    op0=OP.mult), waits=mrow.war() + [tnq])
                                bq.rd.append(tm1)
                                tm2 = E.act(lambda e: e.activation(
                                    out=mrow_bf.t[:], in_=mrow.t[:], func=AF.Sqrt,
                                    bias=0.0, scale=1.0), waits=mrow_bf.war() + [tm1])
                                mrow.rd.append(tm2)
                                for qp in range(4):
                                    tt = E.mm(bbs.t[:, h * 4 + qp:h * 4 + qp + 1],
                                              mrow_bf.t[0:1, qp * 128:(qp + 1) * 128], one1[:],
                                              start=True, stop=True,
                                              waits=(wbias if first_bias else []) + [tm2],
                                              inc=(h == H - 1 and qp == 3))
                                    first_bias = False
                                    mrow_bf.rd.append(tt)
                        bbs.wr = tt
                        tbias = E.act(lambda e: e.activation(out=bias_sb.t[:],
                                                             in_=bbs.t[:, 0:64],
                                                             func=AF.Copy, scale=-SCALE),
                                      waits=bias_sb.war() + [tt])
                        bbs.rd.append(tbias)
                        E.drain("act")
                        tlz = E.dve(lambda e: e.memset(l_col.t[:], 0.0),
                                    waits=l_col.war())
                        E.drain("dve")

                        for j in range(i + 1):
                            diag = (j == i)
                            k0 = j * ST
                            for hf in range(2):
                                ks = knb[(2 * j + hf) % 3]
                                ks.tok = E.dma("sp", ks.t[:],
                                               kn_scr[:, hf * 8:(hf + 1) * 8, k0:k0 + 512],
                                               waits=ks.war())
                                vs = vbuf[(2 * j + hf) % 3]
                                vs.tok = E.dma("sp", vs.t[:],
                                               v_scr[k0 + hf * 256:k0 + (hf + 1) * 256, :]
                                               .rearrange("(pt p) v -> p pt v", p=128),
                                               waits=vs.war())
                            for h in range(H):
                                ks = knb[(2 * j + h // 8) % 3]
                                kcol = (h % 8) * 512
                                rc = 16 + h // 2
                                rof = (h % 2) * 64
                                p_toks = []
                                for qp in range(4):
                                    Nk = (qp + 1) * 128 if diag else 512
                                    bs = B[(qp % 2) if j % 2 == 0 else (6 + qp % 2)]
                                    w0 = bs.war() + [tq, ks.tok, tbias]
                                    E.mm(bs.t[:, 0:Nk],
                                         q_sb.t[:, h * 512 + qp * 128:h * 512 + (qp + 1) * 128],
                                         ks.t[:, kcol:kcol + Nk],
                                         start=True, stop=False, waits=w0)
                                    tgs = E.mm(bs.t[:, 0:Nk],
                                               q_sb.t[rof:rof + 64,
                                                      rc * 512 + qp * 128:rc * 512 + (qp + 1) * 128],
                                               krope2[rof:rof + 64, k0:k0 + Nk],
                                               start=False, stop=True, inc=True)
                                    bs.wr = tgs
                                    ks.rd.append(tgs)
                                    q_sb.rd.append(tgs)
                                    lt = l_tmp[qp % 2]
                                    if diag:
                                        texp = E.act(lambda e, b=bs, n=Nk, idx=h * 4 + qp, q=qp:
                                                     e.activation(
                                                         out=P_sb.t[:, q * 512:q * 512 + n],
                                                         in_=b.t[:, 0:n], func=AF.Exp,
                                                         bias=bias_sb.t[:, idx:idx + 1],
                                                         scale=SCALE),
                                                     waits=P_sb.war() + [tgs])
                                    else:
                                        texp = E.act(lambda e, b=bs, idx=h * 4 + qp, q=qp, l=lt:
                                                     e.activation(
                                                         out=P_sb.t[:, q * 512:(q + 1) * 512],
                                                         in_=b.t[:], func=AF.Exp,
                                                         bias=bias_sb.t[:, idx:idx + 1],
                                                         scale=SCALE, accum_out=l.t[:]),
                                                     waits=P_sb.war() + lt.war() + [tgs])
                                    bias_sb.rd.append(texp)
                                    P_sb.wr = texp
                                    bs.rd.append(texp)
                                    if not diag:
                                        tl = E.dve(lambda e, idx=h * 4 + qp, l=lt: e.tensor_tensor(
                                            out=l_col.t[:, idx:idx + 1],
                                            in0=l_col.t[:, idx:idx + 1], in1=l.t[:],
                                            op=OP.add), waits=[texp, tlz])
                                        lt.rd.append(tl)
                                        l_col.wr = tl
                                    p_toks.append(texp)
                                if diag:
                                    mts = []
                                    for qp in range(4):
                                        c0 = qp * 512 + qp * 128
                                        tmsk = E.dve(lambda e, c=c0: e.tensor_tensor(
                                            out=P_sb.t[:, c:c + 128],
                                            in0=P_sb.t[:, c:c + 128],
                                            in1=tri[:], op=OP.mult), waits=[p_toks[qp]])
                                        mts.append(tmsk)
                                        P_sb.wr = tmsk
                                    E.drain("dve")
                                    rts = []
                                    w4 = l_tmp4.war()
                                    for qp in range(4):
                                        Nk = (qp + 1) * 128
                                        trd = E.dve(lambda e, q=qp, n=Nk: e.reduce_sum(
                                            out=l_tmp4.t[:, q:q + 1],
                                            in_=P_sb.t[:, q * 512:q * 512 + n],
                                            axis=AX.X), waits=w4 if qp == 0 else ())
                                        rts.append(trd)
                                        P_sb.rd.append(trd)
                                    E.drain("dve")
                                    for qp in range(4):
                                        tl = E.dve(lambda e, idx=h * 4 + qp, q=qp: e.tensor_tensor(
                                            out=l_col.t[:, idx:idx + 1],
                                            in0=l_col.t[:, idx:idx + 1],
                                            in1=l_tmp4.t[:, q:q + 1],
                                            op=OP.add), waits=[rts[qp], tlz])
                                        l_tmp4.rd.append(tl)
                                        l_col.wr = tl
                                    p_done = ("dve", E.c["dve"])
                                else:
                                    p_done = ("act", E.c["act"])
                                if "p_dbg" in dump and i == 1 and h == 8 and j == 0:
                                    P_sb.rd.append(E.dma("scq", p_dbg, P_sb.t[:],
                                                         waits=[p_done]))
                                pt_ev = []
                                for ksb in range(4):
                                    bt = B[2 + ksb % 2]
                                    w0 = bt.war() + [p_done]
                                    first = True
                                    tg = None
                                    for qp in (range(ksb, 4) if diag else range(4)):
                                        tg = E.mm(bt.t[:, qp * 128:(qp + 1) * 128],
                                                  P_sb.t[:, qp * 512 + ksb * 128:qp * 512 + (ksb + 1) * 128],
                                                  ident[:], start=True, stop=True,
                                                  waits=w0 if first else (), inc=(qp == 3))
                                        first = False
                                    bt.wr = tg
                                    P_sb.rd.append(tg)
                                    c0 = ksb * 128 if diag else 0
                                    pts = PT_sb[ksb]
                                    tevp = E.act(lambda e, b=bt, c=c0, k=ksb: e.activation(
                                        out=PT_sb[k].t[:, c:512], in_=b.t[:, c:512],
                                        func=AF.Copy), waits=pts.war() + [tg])
                                    bt.rd.append(tevp)
                                    pt_ev.append((ksb, tevp))
                                bo = B[4 + h % 2]
                                w0 = bo.war()
                                tg = None
                                touched = []
                                for idx, (ksb, tevp) in enumerate(pt_ev):
                                    vs = vbuf[(2 * j + ksb // 2) % 3]
                                    c0 = ksb * 128 if diag else 0
                                    tg = E.mm(bo.t[:, c0:512],
                                              vs.t[:, (ksb % 2) * 2048 + h * 128:(ksb % 2) * 2048 + (h + 1) * 128],
                                              PT_sb[ksb].t[:, c0:512],
                                              start=(idx == 0), stop=(idx == len(pt_ev) - 1),
                                              waits=(w0 if idx == 0 else []) + [tevp, vs.tok],
                                              inc=(idx == len(pt_ev) - 1))
                                    touched.append((vs, ksb))
                                for vs, ksb in touched:
                                    vs.rd.append(tg)
                                    PT_sb[ksb].rd.append(tg)
                                bo.wr = tg
                                if j == 0:
                                    to = E.dve(lambda e, hh=h, b=bo: e.tensor_copy(
                                        out=O_sb.t[:, hh * 512:(hh + 1) * 512], in_=b.t[:]),
                                        waits=(O_sb.war() if h == 0 else []) + [tg])
                                else:
                                    to = E.dve(lambda e, hh=h, b=bo: e.tensor_tensor(
                                        out=O_sb.t[:, hh * 512:(hh + 1) * 512],
                                        in0=O_sb.t[:, hh * 512:(hh + 1) * 512],
                                        in1=b.t[:], op=OP.add), waits=[tg])
                                bo.rd.append(to)
                                O_sb.wr = to
                            E.drain("dve")

                        if "o_dbg" in dump and i == 1:
                            O_sb.rd.append(E.dma("scq", o_dbg, O_sb.t[:],
                                                 waits=[("dve", E.c["dve"])]))
                        if "l_dbg" in dump and i == 0:
                            l_col.rd.append(E.dma("scq", l_dbg, l_col.t[:],
                                                  waits=[("dve", E.c["dve"])]))
                        tcb = E.dve(lambda e: e.tensor_copy(out=l_bf.t[:], in_=l_col.t[:]),
                                    waits=l_bf.war())
                        l_col.rd.append(tcb)
                        l_bf.wr = tcb
                        E.drain("dve")
                        blt = B[6]
                        tlt = E.mm(blt.t[0:64, 0:128], l_bf.t[:], ident[:], start=True,
                                   stop=True, waits=blt.war() + [tcb], inc=True)
                        l_bf.rd.append(tlt)
                        blt.wr = tlt
                        tle = E.act(lambda e: e.activation(out=lT.t[:], in_=blt.t[0:64, 0:128],
                                                           func=AF.Copy),
                                    waits=lT.war() + [tlt])
                        blt.rd.append(tle)
                        tlw = E.dma("scq",
                                    lrow_scr.rearrange("a (p c) -> (a p) c", p=64),
                                    lT.t[:], waits=lrow_state.war() + [tle])
                        lT.rd.append(tlw)
                        if "linv_scr" in dump:
                            lT.rd.append(E.dma("scq", linv_scr[i], lT.t[:], waits=[tle]))
                        for h in range(H):
                            lr = lrows[h % 2]
                            tlr = E.dma("sp", lr.t[:],
                                        lrow_scr[:, h * 512:(h + 1) * 512],
                                        waits=lr.war() + [tlw])
                            lrow_state.rd.append(tlr)
                            trc = E.dve(lambda e, l=lr: e.reciprocal(
                                out=rrow_b.t[:], in_=l.t[:]),
                                waits=rrow_b.war() + [tlr])
                            lr.rd.append(trc)
                            E.drain("dve")
                            trcb = E.act(lambda e: e.activation(
                                out=rrow_bb.t[:], in_=rrow_b.t[:], func=AF.Copy),
                                waits=rrow_bb.war() + [trc])
                            rrow_b.rd.append(trcb)
                            bf_ = B[7 if h % 2 == 0 else 3]
                            tfc = E.mm(bf_.t[:], ones_rb[:], rrow_bb.t[:],
                                       start=True, stop=True, waits=bf_.war() + [trcb],
                                       inc=True)
                            bf_.wr = tfc
                            rrow_bb.rd.append(tfc)
                            tob = E.dve(lambda e, hh=h, b=bf_: e.tensor_tensor(
                                out=O_bf.t[:, hh * 512:(hh + 1) * 512],
                                in0=O_sb.t[:, hh * 512:(hh + 1) * 512],
                                in1=b.t[:], op=OP.mult),
                                waits=(O_bf.war() if h == 0 else []) + [tfc, O_sb.wr])
                            bf_.rd.append(tob)
                            O_sb.rd.append(tob)
                            O_bf.wr = tob
                        tok_obf = ("dve", E.c["dve"])
                        if "attn_scr" in dump:
                            O_bf.rd.append(
                                E.dma("scq", attn_scr[:, :, tok0:tok0 + 512],
                                      O_bf.t[:, :].rearrange("p (h t) -> p h t", t=512),
                                      waits=[tok_obf]))

                        for dc in range(4):
                            two = E.dma("sp", wo_dc.t[:, :].rearrange("p (h d) -> p h d", d=512),
                                        woB.rearrange("(h p) d -> p h d", p=128)
                                        [:, :, dc * 512:(dc + 1) * 512], waits=wo_dc.war())
                            txj = E.dma("sp", x_jit.t[:, :].rearrange("p (pt d) -> p pt d", d=512),
                                        x_in[tok0:tok0 + 512, dc * 512:(dc + 1) * 512]
                                        .rearrange("(pt p) d -> p pt d", p=128),
                                        waits=x_jit.war())
                            for qp in range(4):
                                bop = B[qp % 2]
                                w0 = bop.war() + [two, tok_obf]
                                tg = None
                                for h in range(H):
                                    tg = E.mm(bop.t[:],
                                              O_bf.t[:, h * 512 + qp * 128:h * 512 + (qp + 1) * 128],
                                              wo_dc.t[:, h * 512:(h + 1) * 512],
                                              start=(h == 0), stop=(h == H - 1),
                                              waits=w0 if h == 0 else (), inc=(h == H - 1))
                                bop.wr = tg
                                wo_dc.rd.append(tg)
                                O_bf.rd.append(tg)
                                xs1 = x1_st[qp % 2]
                                tadd = E.dve(lambda e, b=bop, q=qp: e.tensor_tensor(
                                    out=xs1.t[:], in0=b.t[:],
                                    in1=x_jit.t[:, q * 512:(q + 1) * 512], op=OP.add),
                                    waits=xs1.war() + [tg, txj])
                                bop.rd.append(tadd)
                                x_jit.rd.append(tadd)
                                xs1.rd.append(
                                    E.dma("scq",
                                          x1_scr[tok0 + qp * 128:tok0 + (qp + 1) * 128,
                                                 dc * 512:(dc + 1) * 512],
                                          xs1.t[:], waits=[tadd]))
                    E.barrier()

            if "C" in phases:
                if True:
                    ab.reset(); af.reset()
                    acc = Bk(af.alloc(128, 8 * D))
                    ln2_fm = Bk(ab.alloc(128, NKC * TP))
                    ln2_tm = [Bk(ab.alloc(128, D)) for k in range(2)]
                    h_buf = Bk(ab.alloc(128, 16 * TP))
                    wg_sb = [Bk(ab.alloc(128, NKC * 128)) for k in range(2)]
                    wu_sb = [Bk(ab.alloc(128, NKC * 128)) for k in range(2)]
                    wd_sb = [Bk(ab.alloc(128, 16 * 512)) for k in range(2)]
                    gs_sb = [Bk(ab.alloc(128, 512)) for k in range(2)]
                    lnst2 = af.alloc(128, 16)
                    xsq2 = Bk(af.alloc(128, D))

                    prev = {}
                    for f in range(NTP):
                        t0 = f * TP
                        acc_t = []
                        war = acc.war()
                        for pt in range(8):
                            acc_t.append(E.dma("sp", acc.t[:, pt * D:(pt + 1) * D],
                                               x1_scr[t0 + pt * 128:t0 + (pt + 1) * 128, :],
                                               waits=war if pt == 0 else ()))
                        for pt in range(8):
                            xa = acc.t[:, pt * D:(pt + 1) * D]
                            t1 = E.dve(lambda e, x=xa: e.reduce_sum(out=lnst2[:, 0:1], in_=x,
                                                                    axis=AX.X),
                                       waits=[acc_t[pt], prev.get("c4")])
                            t2 = E.act(lambda e, x=xa: e.activation(out=xsq2.t[:], in_=x,
                                                                    func=AF.Square),
                                       waits=xsq2.war() + [acc_t[pt]])
                            t3 = E.dve(lambda e: e.reduce_sum(out=lnst2[:, 1:2], in_=xsq2.t[:],
                                                              axis=AX.X), waits=[t2])
                            xsq2.rd.append(t3)
                            E.drain("dve")
                            t4 = E.dve(lambda e: e.tensor_scalar(
                                out=lnst2[:, 2:3], in0=lnst2[:, 0:1], scalar1=-1.0 / D,
                                scalar2=None, op0=OP.mult), waits=[prev.get("c5")])
                            prev["c4"] = t4
                            E.drain("dve")
                            t5 = E.act(lambda e: e.activation(out=lnst2[:, 3:4], in_=lnst2[:, 2:3],
                                                              func=AF.Square),
                                       waits=[t4, prev.get("c6")])
                            prev["c5"] = t5
                            t6 = E.dve(lambda e: e.scalar_tensor_tensor(
                                out=lnst2[:, 4:5], in0=lnst2[:, 1:2], scalar=1.0 / D,
                                in1=lnst2[:, 3:4], op0=OP.mult, op1=OP.subtract),
                                waits=[t3, t5])
                            prev["c6"] = t6
                            t7 = E.act(lambda e: e.activation(out=lnst2[:, 5:6], in_=lnst2[:, 4:5],
                                                              func=AF.Sqrt, bias=LN_EPS,
                                                              scale=1.0),
                                       waits=[t6, prev.get("c8")])
                            t8 = E.dve(lambda e: e.reciprocal(out=lnst2[:, 6:7],
                                                              in_=lnst2[:, 5:6]), waits=[t7])
                            prev["c8"] = t8
                            E.drain("dve")
                            ls = ln2_tm[pt % 2]
                            t9 = E.dve(lambda e, l=ls, x=xa: e.tensor_scalar(
                                out=l.t[:], in0=x, scalar1=lnst2[:, 2:3],
                                scalar2=lnst2[:, 6:7], op0=OP.add, op1=OP.mult),
                                waits=ls.war() + [t8])
                            acc.rd.append(t9)
                            E.drain("dve")
                            for kcg in range(4):
                                bk = B[kcg % 2]
                                w0 = bk.war() + ln2_fm.war() + [t9]
                                tg = None
                                for jj in range(4):
                                    kc = kcg * 4 + jj
                                    tg = E.mm(bk.t[:, jj * 128:(jj + 1) * 128],
                                              ls.t[:, kc * 128:(kc + 1) * 128], ident[:],
                                              start=True, stop=True,
                                              waits=w0 if jj == 0 else (), inc=(jj == 3))
                                bk.wr = tg
                                ls.rd.append(tg)
                                out_v = ln2_fm.t[:, :].rearrange("p (kc t) -> p kc t", t=TP)[
                                    :, kcg * 4:(kcg + 1) * 4, pt * 128:(pt + 1) * 128]
                                te = E.act(lambda e, b=bk, o=out_v: e.activation(
                                    out=o, in_=b.t[:, :].rearrange("p (j m) -> p j m", m=128),
                                    func=AF.Copy), waits=[tg])
                                bk.rd.append(te)
                                ln2_fm.wr = te
                        tok_ln2 = ("act", E.c["act"])
                        if "ln2_dbg" in dump:
                            ln2_fm.rd.append(E.dma("scq", ln2_dbg[f], ln2_fm.t[:],
                                                   waits=[tok_ln2]))

                        for w in range(4):
                            hwar = h_buf.war()
                            for ii in range(16):
                                ic = w * 16 + ii
                                wgs = wg_sb[ii % 2]
                                twg = E.dma("sp", wgs.t[:],
                                            wguB[:, :, ic * 128:(ic + 1) * 128]
                                            .rearrange("kc p m -> p kc m"), waits=wgs.war())
                                wus = wu_sb[ii % 2]
                                twu = E.dma("sp", wus.t[:],
                                            wguB[:, :, INTER + ic * 128:INTER + (ic + 1) * 128]
                                            .rearrange("kc p m -> p kc m"), waits=wus.war())
                                for th in range(2):
                                    bg = B[2 + th]
                                    w0 = bg.war() + [twg, tok_ln2]
                                    tg = None
                                    for kc in range(NKC):
                                        tg = E.mm(bg.t[:], wgs.t[:, kc * 128:(kc + 1) * 128],
                                                  ln2_fm.t[:, kc * TP + th * 512:kc * TP + (th + 1) * 512],
                                                  start=(kc == 0), stop=(kc == NKC - 1),
                                                  waits=w0 if kc == 0 else (),
                                                  inc=(kc == NKC - 1))
                                    bg.wr = tg
                                    wgs.rd.append(tg)
                                    ln2_fm.rd.append(tg)
                                    bu = B[4 + th]
                                    w0 = bu.war() + [twu, tok_ln2]
                                    tu = None
                                    for kc in range(NKC):
                                        tu = E.mm(bu.t[:], wus.t[:, kc * 128:(kc + 1) * 128],
                                                  ln2_fm.t[:, kc * TP + th * 512:kc * TP + (th + 1) * 512],
                                                  start=(kc == 0), stop=(kc == NKC - 1),
                                                  waits=w0 if kc == 0 else (),
                                                  inc=(kc == NKC - 1))
                                    bu.wr = tu
                                    wus.rd.append(tu)
                                    ln2_fm.rd.append(tu)
                                    gs = gs_sb[th]
                                    tsi = E.act(lambda e, b=bg, g=gs: e.activation(
                                        out=g.t[:], in_=b.t[:], func=AF.Silu),
                                        waits=gs.war() + [tg])
                                    bg.rd.append(tsi)
                                    tmu = E.dve(lambda e, g=gs, b=bu, i2=ii, t2=th:
                                                e.tensor_tensor(
                                                    out=h_buf.t[:, i2 * TP + t2 * 512:i2 * TP + (t2 + 1) * 512],
                                                    in0=g.t[:], in1=b.t[:], op=OP.mult),
                                                waits=hwar + [tsi, tu])
                                    gs.rd.append(tmu)
                                    bu.rd.append(tmu)
                                    h_buf.wr = tmu
                            tok_h = ("dve", E.c["dve"])
                            for dc in range(4):
                                wds = wd_sb[dc % 2]
                                twd = E.dma("sp", wds.t[:],
                                            wdnB[w * 2048:(w + 1) * 2048, dc * 512:(dc + 1) * 512]
                                            .rearrange("(ii p) d -> p ii d", p=128),
                                            waits=wds.war())
                                for ts2 in range(8):
                                    bd = B[6 + ts2 % 2]
                                    w0 = bd.war() + [twd, tok_h]
                                    tg = None
                                    for ii in range(16):
                                        tg = E.mm(bd.t[:],
                                                  h_buf.t[:, ii * TP + ts2 * 128:ii * TP + (ts2 + 1) * 128],
                                                  wds.t[:, ii * 512:(ii + 1) * 512],
                                                  start=(ii == 0), stop=(ii == 15),
                                                  waits=w0 if ii == 0 else (), inc=(ii == 15))
                                    bd.wr = tg
                                    wds.rd.append(tg)
                                    h_buf.rd.append(tg)
                                    tac = E.dve(lambda e, b=bd, t2=ts2, d=dc: e.tensor_tensor(
                                        out=acc.t[:, t2 * D + d * 512:t2 * D + (d + 1) * 512],
                                        in0=acc.t[:, t2 * D + d * 512:t2 * D + (d + 1) * 512],
                                        in1=b.t[:], op=OP.add), waits=[tg])
                                    bd.rd.append(tac)
                                    acc.rd.append(tac)
                            E.drain("dve")
                        tok_acc = ("dve", E.c["dve"])
                        for pt in range(8):
                            acc.rd.append(
                                E.dma("scq", y_out[t0 + pt * 128:t0 + (pt + 1) * 128, :],
                                      acc.t[:, pt * D:(pt + 1) * D], waits=[tok_acc]))
                    E.barrier()

        @blk.sync
        def _(eng):
            schedule(Emitter(nc, "sp", eng, sems))

        @blk.gpsimd
        def _(eng):
            schedule(Emitter(nc, "pool", eng, sems))

        @blk.vector
        def _(eng):
            schedule(Emitter(nc, "dve", eng, sems))

        @blk.scalar
        def _(eng):
            schedule(Emitter(nc, "act", eng, sems))

        @blk.tensor
        def _(eng):
            schedule(Emitter(nc, "te", eng, sems))

    return nc



def _host_constants():
    c = {}
    c["ident_bf"] = np.eye(128, dtype=np.float32).astype(BFDT)
    permm = np.zeros((128, 128), dtype=np.float32)
    for r in range(128):
        s = r + 32 if (r // 32) % 2 == 0 else r - 32
        permm[s, r] = 1.0
    c["perm_bf"] = permm.astype(BFDT)
    dupm = np.zeros((64, 128), dtype=np.float32)
    for m in range(128):
        dupm[m % 64, m] = 1.0
    c["dup_bf"] = dupm.astype(BFDT)
    q_i = np.arange(128)[:, None]
    k_i = np.arange(128)[None, :]
    c["tri_bf"] = (k_i <= q_i).astype(np.float32).astype(BFDT)
    c["i128_f32"] = np.eye(128, dtype=np.float32)
    j = np.arange(0, ROPE, 2, dtype=np.float32) / np.float32(ROPE)
    freqs = (1.0 / (np.float32(10000.0) ** j)).astype(np.float32)
    t = np.arange(S, dtype=np.float32)[:, None] * freqs[None, :]
    cos = np.cos(t).astype(np.float32).T
    sin = np.sin(t).astype(np.float32).T
    c["cosq"] = np.ascontiguousarray(np.concatenate([cos, cos, cos, cos], 0))
    c["sinq"] = np.ascontiguousarray(np.concatenate([-sin, sin, -sin, sin], 0))
    return c


def _perm_inputs(Wq, Wkva, Wkvb, kvw, Wgu, Wdown):
    cols = []
    for h in range(H):
        cols.extend(range(h * 192, h * 192 + 128))
    for h in range(H):
        cols.extend(range(h * 192 + 128, h * 192 + 192, 2))
        cols.extend(range(h * 192 + 129, h * 192 + 192, 2))
    wq_p = np.ascontiguousarray(Wq[:, cols])
    cols = list(range(512)) + list(range(512, 576, 2)) + list(range(513, 576, 2))
    wkva_p = np.ascontiguousarray(Wkva[:, cols])
    cols = []
    for h in range(H):
        cols.extend(range(h * 256, h * 256 + 128))
    for h in range(H):
        cols.extend(range(h * 256 + 128, h * 256 + 256))
    wkvb_p = np.ascontiguousarray(Wkvb[:, cols])
    kvw_t = np.ascontiguousarray(kvw.reshape(4, 128).T)
    wguT = np.ascontiguousarray(Wgu.T)
    wdownT = np.ascontiguousarray(Wdown.T)
    return wq_p, wkva_p, wkvb_p, kvw_t, wguT, wdownT


def make_inputs(x_b, Wq, Wkva, kvw, Wkvb, Wo, Wgu, Wdown):
    wq_p, wkva_p, wkvb_p, kvw_t, wguT, wdownT = _perm_inputs(
        np.asarray(Wq, np.float32), np.asarray(Wkva, np.float32),
        np.asarray(Wkvb, np.float32), np.asarray(kvw, np.float32),
        np.asarray(Wgu, np.float32), np.asarray(Wdown, np.float32))
    base = {"wq_p": wq_p, "wkva_p": wkva_p, "wkvb_p": wkvb_p, "kvw": kvw_t,
            "wo": np.ascontiguousarray(np.asarray(Wo, np.float32)),
            "wguT": wguT, "wdownT": wdownT, **_host_constants()}
    base["x"] = np.ascontiguousarray(np.asarray(x_b, np.float32))
    return base


def kernel(L0_attn_x, L0_attn_Wq, L0_attn_Wkv_a, L0_attn_kv_norm_w, L0_attn_Wkv_b,
           L0_attn_Wo, L0_ffn_w_gate_up, L0_ffn_w_down):
    x = np.asarray(L0_attn_x, dtype=np.float32)
    base = make_inputs(x[0], L0_attn_Wq, L0_attn_Wkv_a, L0_attn_kv_norm_w,
                       L0_attn_Wkv_b, L0_attn_Wo, L0_ffn_w_gate_up, L0_ffn_w_down)
    nc = build_program()
    in_maps = []
    for b in range(8):
        m = dict(base)
        m["x"] = np.ascontiguousarray(x[b])
        in_maps.append(m)
    res = run_bass_kernel_spmd(nc, in_maps, core_ids=list(range(8)))
    return np.stack([np.asarray(r["y"], np.float32) for r in res.results], axis=0)


# version 04
# speedup vs baseline: 1.1638x; 1.1638x over previous
import contextlib
import numpy as np
import ml_dtypes

import concourse.bass as bass
import concourse.mybir as mybir
from concourse.bass_utils import run_bass_kernel_spmd
from concourse.alu_op_type import AluOpType as OP

F32 = mybir.dt.float32
BF16 = mybir.dt.bfloat16
AF = mybir.ActivationFunctionType
AX = mybir.AxisListType
BFDT = ml_dtypes.bfloat16

S = 4096
D = 2048
H = 16
ROPE = 64
LORA = 512
INTER = 8192
ST = 512
NST = S // ST
NKC = D // 128
NQC = 24
SCALE = float(np.float32(1.0 / np.sqrt(192.0)))
LN_EPS = 1e-5
RMS_EPS = 1.1920928955078125e-07
TP = 1024
NTP = S // TP

ENGS = ("te", "act", "dve", "pool", "sp")
QENG = {"sp": "sp", "scq": "act", "g": "pool"}


class Bk:
    __slots__ = ("t", "rd", "wr", "tok")

    def __init__(self, t):
        self.t = t
        self.rd = []
        self.wr = None
        self.tok = None

    def war(self):
        w = list(self.rd)
        self.rd = []
        return w


class Arena:

    def __init__(self, t, ncols):
        self.t = t
        self.n = ncols
        self.off = 0

    def reset(self):
        self.off = 0

    def alloc(self, p, cols):
        off = (self.off + 15) // 16 * 16
        assert off + cols <= self.n, f"arena overflow {off}+{cols}>{self.n}"
        self.off = off + cols
        return self.t[0:p, off:off + cols]


class Emitter:
    def __init__(self, nc, engine_name, handle, sems):
        self.nc = nc
        self.name = engine_name
        self.h = handle
        self.sems = sems
        self.c = {k: 0 for k in sems}
        self.sat = {}
        self.nbar = 0

    def _wait(self, tok):
        if tok is None:
            return
        sem, val = tok
        if self.sat.get(sem, 0) >= val:
            return
        self.h.wait_ge(self.sems[sem], val)
        self.sat[sem] = val

    def do(self, eng, fn, waits=(), inc=None, amt=1, fence=False):
        tok = None
        if inc is not None:
            self.c[inc] += amt
            tok = (inc, self.c[inc])
        if self.name == eng:
            for w in waits:
                self._wait(w)
            inst = fn(self.h)
            if inc is not None:
                if fence:
                    self.h.drain()
                    self.h.sem_inc(self.sems[inc], amt)
                else:
                    inst.then_inc(self.sems[inc], amt)
        return tok

    def dma(self, queue, out, in_, waits=()):
        return self.do(QENG[queue], lambda e: e.dma_start(out=out, in_=in_),
                       waits=waits, inc=queue, amt=16)

    def drain(self, eng):
        self.do(eng, lambda e: e.drain())

    def barrier(self):
        self.nbar += 1
        n = self.nbar * len(ENGS)
        for q, e in QENG.items():
            if e == self.name:
                self._wait((q, self.c[q]))
        if self.name != "sp":
            self.h.drain()
        self.h.sem_inc(self.sems["bar"], 1)
        self.c["bar"] = n
        self.h.wait_ge(self.sems["bar"], n)
        self.sat = {"bar": n}

    def mm(self, out, lhsT, rhs, start, stop, waits=(), inc=False):
        return self.do("te", lambda e: e.matmul(out, lhsT, rhs, start=start, stop=stop),
                       waits=waits, inc=("pe" if inc else None))

    def act(self, fn, waits=()):
        return self.do("act", fn, waits=waits, inc="act")

    def dve(self, fn, waits=()):
        return self.do("dve", fn, waits=waits, inc="dve")

    def pool(self, fn, waits=()):
        return self.do("pool", fn, waits=waits, inc="pool")


def build_program(phases="0ABC", dump=()):
    nc = bass.Bass("TRN2", target_bir_lowering=False)

    def dram(name, shape, dt, kind):
        if name in dump and kind == "Internal":
            kind = "ExternalOutput"
        return nc.dram_tensor(name, shape, dt, kind=kind).ap()

    x_in = dram("x", [S, D], F32, "ExternalInput")
    wq_in = dram("wq_p", [D, 3072], F32, "ExternalInput")
    wkva_in = dram("wkva_p", [D, 576], F32, "ExternalInput")
    wkvb_in = dram("wkvb_p", [LORA, 4096], F32, "ExternalInput")
    kvw_in = dram("kvw", [128, 4], F32, "ExternalInput")
    wo_in = dram("wo", [D, D], F32, "ExternalInput")
    wgu_in = dram("wguT", [D, 2 * INTER], F32, "ExternalInput")
    wdn_in = dram("wdownT", [INTER, D], F32, "ExternalInput")
    ident_in = dram("ident_bf", [128, 128], BF16, "ExternalInput")
    perm_in = dram("perm_bf", [128, 128], BF16, "ExternalInput")
    dup_in = dram("dup_bf", [64, 128], BF16, "ExternalInput")
    tri_in = dram("tri_bf", [128, 128], BF16, "ExternalInput")
    i128f_in = dram("i128_f32", [128, 128], F32, "ExternalInput")
    cos_in = dram("cosq", [128, S], F32, "ExternalInput")
    sin_in = dram("sinq", [128, S], F32, "ExternalInput")

    wqB = dram("wqB", [NQC, 128, D], BF16, "Internal")
    wguB = dram("wguB", [NKC, 128, 2 * INTER], BF16, "Internal")
    wdnB = dram("wdnB", [INTER, D], BF16, "Internal")
    woB = dram("woB", [D, D], BF16, "Internal")
    q_scr = dram("q_scr", [NQC, 128, S], BF16, "Internal")
    kn_scr = dram("kn_scr", [128, H, S], BF16, "Internal")
    v_scr = dram("v_scr", [S, D], BF16, "Internal")
    x1_scr = dram("x1_scr", [S, D], F32, "Internal")
    lrow_scr = dram("lrow_scr", [1, 8192], F32, "Internal")
    attn_scr = dram("attn_scr", [128, H, S], BF16, "Internal")
    p_dbg = dram("p_dbg", [128, 2048], BF16, "Internal")
    o_dbg = dram("o_dbg", [128, H * 512], F32, "Internal")
    ln2_dbg = dram("ln2_dbg", [NTP, 128, NKC * TP], BF16, "Internal")
    l_dbg = dram("l_dbg", [128, 64], F32, "Internal")
    linv_scr = dram("linv_scr", [NST, 64, 128], F32, "Internal")
    y_out = dram("y", [S, D], F32, "ExternalOutput")

    with contextlib.ExitStack() as top:
        E_ = top.enter_context
        ident = E_(nc.sbuf_tensor("ident", [128, 128], BF16))
        perm = E_(nc.sbuf_tensor("perm", [128, 128], BF16))
        dup = E_(nc.sbuf_tensor("dup", [64, 128], BF16))
        tri = E_(nc.sbuf_tensor("tri", [128, 128], BF16))
        i128f = E_(nc.sbuf_tensor("i128f", [128, 128], F32))
        ones_c = E_(nc.sbuf_tensor("ones_c", [128, 1], BF16))
        ones_rf = E_(nc.sbuf_tensor("ones_rf", [1, 128], F32))
        ones_rb = E_(nc.sbuf_tensor("ones_rb", [1, 128], BF16))
        one1 = E_(nc.sbuf_tensor("one1", [1, 1], BF16))
        kvw_sb = E_(nc.sbuf_tensor("kvw_sb", [128, 4], F32))
        nkmax = E_(nc.sbuf_tensor("nkmax", [1, 16], F32))
        krope2 = E_(nc.sbuf_tensor("krope2", [128, S], BF16))
        arena_bf_t = E_(nc.sbuf_tensor("arena_bf", [128, 63488], BF16))
        arena_f_t = E_(nc.sbuf_tensor("arena_f", [128, 18784], F32))
        const_sb = E_(nc.sbuf_tensor("const_sb", [128, 8], F32))
        CONSTS = [0.0, 1.0, LN_EPS, RMS_EPS, 1.0 / LORA, SCALE, -SCALE]
        pbt = [E_(nc.psum_tensor(f"pb{i}", [128, 512], F32)) for i in range(8)]
        sems = {k: E_(nc.semaphore(f"sem_{k}"))
                for k in ["sp", "scq", "g", "pe", "act", "dve", "pool", "bar"]}
        blk = E_(nc.Block())

        def schedule(E: Emitter):
            B = [Bk(t) for t in pbt]
            ab = Arena(arena_bf_t, 63488)
            af = Arena(arena_f_t, 18784)

            E.dma("g", ident[:], ident_in)
            E.dma("g", perm[:], perm_in)
            E.dma("g", dup[:], dup_in)
            E.dma("g", tri[:], tri_in)
            E.dma("g", i128f[:], i128f_in)
            E.dma("g", kvw_sb[:], kvw_in)
            E.pool(lambda e: e.memset(ones_c[:], 1.0))
            E.pool(lambda e: e.memset(ones_rf[:], 1.0))
            E.pool(lambda e: e.memset(ones_rb[:], 1.0))
            E.pool(lambda e: e.memset(one1[:], 1.0))
            E.pool(lambda e: e.memset(nkmax[:], 0.0))
            for ci, cv in enumerate(CONSTS):
                E.pool(lambda e, i2=ci, v=cv: e.memset(const_sb[:, i2:i2 + 1], v))
            if E.name == "sp":
                for ci, cv in enumerate(CONSTS):
                    nc.const_aps.register(F32, cv, const_sb[:, ci:ci + 1])
            E.barrier()

            if "0" in phases:
                if True:
                    ab.reset(); af.reset()
                    stq = [Bk(ab.alloc(128, 3072)) for k in range(2)]
                    stg = [Bk(ab.alloc(128, 8192)) for k in range(2)]
                    std = [Bk(ab.alloc(128, 2048)) for k in range(2)]
                    for kc in range(NKC):
                        s = stq[kc % 2]
                        tin = E.dma("g", s.t[:], wq_in[kc * 128:(kc + 1) * 128, :], waits=s.war())
                        for qc in range(NQC):
                            s.rd.append(E.dma("scq", wqB[qc, :, kc * 128:(kc + 1) * 128],
                                              s.t[:, qc * 128:(qc + 1) * 128], waits=[tin]))
                    for kc in range(NKC):
                        for hf in range(2):
                            s = stg[(kc * 2 + hf) % 2]
                            cols = slice(hf * 8192, (hf + 1) * 8192)
                            tin = E.dma("g", s.t[:], wgu_in[kc * 128:(kc + 1) * 128, cols],
                                        waits=s.war())
                            s.rd.append(E.dma("scq", wguB[kc, :, cols], s.t[:], waits=[tin]))
                    for r in range(INTER // 128):
                        s = std[r % 2]
                        tin = E.dma("g", s.t[:], wdn_in[r * 128:(r + 1) * 128, :], waits=s.war())
                        s.rd.append(E.dma("scq", wdnB[r * 128:(r + 1) * 128, :], s.t[:],
                                          waits=[tin]))
                    for r in range(D // 128):
                        s = std[r % 2]
                        tin = E.dma("g", s.t[:], wo_in[r * 128:(r + 1) * 128, :], waits=s.war())
                        s.rd.append(E.dma("scq", woB[r * 128:(r + 1) * 128, :], s.t[:],
                                          waits=[tin]))
                    E.barrier()

            if "A" in phases:
                if True:
                    ab.reset(); af.reset()
                    wkva = ab.alloc(128, NKC * 576)
                    wkvb_kn = ab.alloc(128, 4 * 2048)
                    wkvb_v = ab.alloc(128, 4 * 2048)
                    x_pt = [Bk(af.alloc(128, D)) for k in range(2)]
                    ln_tm = [Bk(ab.alloc(128, D)) for k in range(2)]
                    ln_fm = Bk(ab.alloc(128, NKC * 512))
                    wq_qc = [Bk(ab.alloc(128, D)) for k in range(2)]
                    q_st = [Bk(ab.alloc(128, 512)) for k in range(2)]
                    tmp_bf = [Bk(ab.alloc(128, 512)) for k in range(2)]
                    a_f32 = af.alloc(128, 512)
                    b_f32 = af.alloc(128, 512)
                    ckv_f = Bk(af.alloc(128, 4 * 512))
                    ckv_b = Bk(ab.alloc(128, 4 * 512))
                    sq_bf = [Bk(ab.alloc(128, 512)) for k in range(2)]
                    sq_kr = Bk(ab.alloc(64, 512))
                    kn_st = Bk(ab.alloc(128, H * 512))
                    v_st = [Bk(ab.alloc(128, 2048)) for k in range(2)]
                    cos_sl = [Bk(af.alloc(128, 512)) for k in range(2)]
                    sin_sl = [Bk(af.alloc(128, 512)) for k in range(2)]
                    rrow = [Bk(af.alloc(1, 512)) for k in range(4)]
                    rrbf = Bk(ab.alloc(1, 512))
                    nmx = Bk(af.alloc(1, 16))
                    lnst = af.alloc(128, 16)
                    xsq = Bk(af.alloc(128, D))

                    tkva = None
                    for kc in range(NKC):
                        tkva = E.dma("g", wkva[:, kc * 576:(kc + 1) * 576],
                                     wkva_in[kc * 128:(kc + 1) * 128, :])
                    tkvb = None
                    for lc in range(4):
                        tkvb = E.dma("g", wkvb_kn[:, lc * 2048:(lc + 1) * 2048],
                                     wkvb_in[lc * 128:(lc + 1) * 128, 0:2048])
                        tkvb = E.dma("g", wkvb_v[:, lc * 2048:(lc + 1) * 2048],
                                     wkvb_in[lc * 128:(lc + 1) * 128, 2048:4096])
                    tfold = None
                    for lc in range(4):
                        for buf in (wkvb_kn, wkvb_v):
                            tfold = E.dve(lambda e, b=buf, l=lc: e.tensor_scalar(
                                out=b[:, l * 2048:(l + 1) * 2048],
                                in0=b[:, l * 2048:(l + 1) * 2048],
                                scalar1=kvw_sb[:, l:l + 1], scalar2=None, op0=OP.mult),
                                waits=[tkvb])
                    E.drain("dve")

                    prev = {}
                    for i in range(NST):
                        tok0 = i * ST
                        cs = cos_sl[i % 2]
                        sn = sin_sl[i % 2]
                        tcos = E.dma("sp", cs.t[:], cos_in[:, tok0:tok0 + 512], waits=cs.war())
                        tsin = E.dma("sp", sn.t[:], sin_in[:, tok0:tok0 + 512], waits=sn.war())

                        for pt in range(4):
                            xs = x_pt[pt % 2]
                            tx = E.dma("sp", xs.t[:],
                                       x_in[tok0 + pt * 128:tok0 + (pt + 1) * 128, :],
                                       waits=xs.war())
                            t1 = E.dve(lambda e, x=xs: e.reduce_sum(out=lnst[:, 0:1], in_=x.t[:],
                                                                    axis=AX.X), waits=[tx])
                            t2 = E.act(lambda e, x=xs: e.activation(out=xsq.t[:], in_=x.t[:],
                                                                    func=AF.Square),
                                       waits=xsq.war() + [tx])
                            xs.rd += [t1, t2]
                            t3 = E.dve(lambda e: e.reduce_sum(out=lnst[:, 1:2], in_=xsq.t[:],
                                                              axis=AX.X), waits=[t2])
                            xsq.rd.append(t3)
                            E.drain("dve")
                            t4 = E.dve(lambda e: e.tensor_scalar(
                                out=lnst[:, 2:3], in0=lnst[:, 0:1], scalar1=-1.0 / D,
                                scalar2=None, op0=OP.mult), waits=[prev.get("ln5")])
                            E.drain("dve")
                            t5 = E.act(lambda e: e.activation(out=lnst[:, 3:4], in_=lnst[:, 2:3],
                                                              func=AF.Square),
                                       waits=[t4, prev.get("ln6")])
                            prev["ln5"] = t5
                            t6 = E.dve(lambda e: e.scalar_tensor_tensor(
                                out=lnst[:, 4:5], in0=lnst[:, 1:2], scalar=1.0 / D,
                                in1=lnst[:, 3:4], op0=OP.mult, op1=OP.subtract),
                                waits=[t3, t5])
                            prev["ln6"] = t6
                            t7 = E.act(lambda e: e.activation(out=lnst[:, 5:6], in_=lnst[:, 4:5],
                                                              func=AF.Sqrt, bias=LN_EPS,
                                                              scale=1.0),
                                       waits=[t6, prev.get("ln8")])
                            t8 = E.dve(lambda e: e.reciprocal(out=lnst[:, 6:7], in_=lnst[:, 5:6]),
                                       waits=[t7])
                            prev["ln8"] = t8
                            E.drain("dve")
                            ls = ln_tm[pt % 2]
                            t9 = E.dve(lambda e, l=ls, x=xs: e.tensor_scalar(
                                out=l.t[:], in0=x.t[:], scalar1=lnst[:, 2:3],
                                scalar2=lnst[:, 6:7], op0=OP.add, op1=OP.mult),
                                waits=ls.war() + [t8])
                            xs.rd.append(t9)
                            E.drain("dve")
                            for kcg in range(4):
                                bk = B[kcg % 2]
                                w0 = bk.war() + ln_fm.war() + [t9]
                                tg = None
                                for j in range(4):
                                    kc = kcg * 4 + j
                                    tg = E.mm(bk.t[:, j * 128:(j + 1) * 128],
                                              ls.t[:, kc * 128:(kc + 1) * 128], ident[:],
                                              start=True, stop=True,
                                              waits=w0 if j == 0 else (), inc=(j == 3))
                                bk.wr = tg
                                ls.rd.append(tg)
                                out_v = ln_fm.t[:, :].rearrange("p (kc t) -> p kc t", t=512)[
                                    :, kcg * 4:(kcg + 1) * 4, pt * 128:(pt + 1) * 128]
                                te = E.act(lambda e, b=bk, o=out_v: e.activation(
                                    out=o, in_=b.t[:, :].rearrange("p (j m) -> p j m", m=128),
                                    func=AF.Copy), waits=[tg])
                                bk.rd.append(te)
                                ln_fm.wr = te
                        tok_lnfm = ("act", E.c["act"])

                        for qc in range(NQC):
                            ws = wq_qc[qc % 2]
                            tw = E.dma("sp", ws.t[:], wqB[qc, :, :], waits=ws.war())
                            bk = B[2 + qc % 2]
                            w0 = bk.war() + [tw, tok_lnfm]
                            tg = None
                            for kc in range(NKC):
                                tg = E.mm(bk.t[:], ws.t[:, kc * 128:(kc + 1) * 128],
                                          ln_fm.t[:, kc * 512:(kc + 1) * 512],
                                          start=(kc == 0), stop=(kc == NKC - 1),
                                          waits=w0 if kc == 0 else (), inc=(kc == NKC - 1))
                            bk.wr = tg
                            ws.rd.append(tg)
                            ln_fm.rd.append(tg)
                            qs = q_st[qc % 2]
                            if qc < 16:
                                tq = E.act(lambda e, b=bk, q=qs: e.activation(
                                    out=q.t[:], in_=b.t[:], func=AF.Copy),
                                    waits=qs.war() + [tg])
                                bk.rd.append(tq)
                            else:
                                ts = tmp_bf[qc % 2]
                                tev = E.act(lambda e, b=bk, t=ts: e.activation(
                                    out=t.t[:], in_=b.t[:], func=AF.Copy),
                                    waits=ts.war() + [tg])
                                bk.rd.append(tev)
                                bp = B[4 + qc % 2]
                                wp = bp.war() + [tev]
                                tp = E.mm(bp.t[:], perm[:], ts.t[:], start=True, stop=True,
                                          waits=wp, inc=True)
                                bp.wr = tp
                                ts.rd.append(tp)
                                ta = E.dve(lambda e, t=ts, c=cs: e.tensor_tensor(
                                    out=a_f32[:], in0=t.t[:], in1=c.t[:], op=OP.mult),
                                    waits=[tev, tcos])
                                ts.rd.append(ta)
                                cs.rd.append(ta)
                                tb = E.dve(lambda e, b=bp, s=sn: e.tensor_tensor(
                                    out=b_f32[:], in0=b.t[:], in1=s.t[:], op=OP.mult),
                                    waits=[tp, tsin])
                                bp.rd.append(tb)
                                sn.rd.append(tb)
                                E.drain("dve")
                                tq = E.dve(lambda e, q=qs: e.tensor_tensor(
                                    out=q.t[:], in0=a_f32[:], in1=b_f32[:], op=OP.add),
                                    waits=qs.war())
                                E.drain("dve")
                            qs.rd.append(E.dma("scq", q_scr[qc, :, tok0:tok0 + 512], qs.t[:],
                                               waits=[tq]))

                        br = B[6]
                        wbr = br.war()
                        tgr = None
                        for oc in range(4):
                            bk = B[2 + oc % 2]
                            w0 = bk.war() + [tok_lnfm, tkva]
                            tg = None
                            for kc in range(NKC):
                                tg = E.mm(bk.t[:],
                                          wkva[:, kc * 576 + oc * 128:kc * 576 + (oc + 1) * 128],
                                          ln_fm.t[:, kc * 512:(kc + 1) * 512],
                                          start=(kc == 0), stop=(kc == NKC - 1),
                                          waits=w0 if kc == 0 else (), inc=(kc == NKC - 1))
                            bk.wr = tg
                            ln_fm.rd.append(tg)
                            tev = E.act(lambda e, b=bk, o=oc: e.activation(
                                out=ckv_f.t[:, o * 512:(o + 1) * 512], in_=b.t[:],
                                func=AF.Copy), waits=(ckv_f.war() if oc == 0 else []) + [tg])
                            bk.rd.append(tev)
                            ckv_f.wr = tev
                            sq = sq_bf[oc % 2]
                            tsq = E.act(lambda e, b=bk, s=sq: e.activation(
                                out=s.t[:], in_=b.t[:], func=AF.Square),
                                waits=sq.war() + [tg])
                            bk.rd.append(tsq)
                            tgr = E.mm(br.t[0:1, :], ones_c[:], sq.t[:],
                                       start=(oc == 0), stop=(oc == 3),
                                       waits=(wbr if oc == 0 else []) + [tsq], inc=(oc == 3))
                            sq.rd.append(tgr)
                        br.wr = tgr
                        r0 = rrow[0]
                        tsd = E.act(lambda e, r=r0: e.activation(
                            out=r.t[:], in_=br.t[0:1, :], func=AF.Sqrt,
                            bias=RMS_EPS, scale=1.0 / LORA), waits=r0.war() + [tgr])
                        br.rd.append(tsd)
                        r1 = rrow[1]
                        trr = E.dve(lambda e, a=r0, b=r1: e.reciprocal(out=b.t[:], in_=a.t[:]),
                                    waits=r1.war() + [tsd])
                        r0.rd.append(trr)
                        trb = E.act(lambda e, b=r1: e.activation(out=rrbf.t[:], in_=b.t[:],
                                                                 func=AF.Copy),
                                    waits=rrbf.war() + [trr])
                        r1.rd.append(trb)
                        bb = B[5]
                        tbc = E.mm(bb.t[:], ones_rb[:], rrbf.t[:], start=True, stop=True,
                                   waits=bb.war() + [trb], inc=True)
                        bb.wr = tbc
                        rrbf.rd.append(tbc)
                        tok_ckv = None
                        for oc in range(4):
                            tn = E.dve(lambda e, o=oc, b=bb: e.tensor_tensor(
                                out=ckv_b.t[:, o * 512:(o + 1) * 512],
                                in0=ckv_f.t[:, o * 512:(o + 1) * 512],
                                in1=b.t[:], op=OP.mult),
                                waits=(ckv_b.war() if oc == 0 else []) + [tbc, ckv_f.wr])
                            bb.rd.append(tn)
                            ckv_f.rd.append(tn)
                            tok_ckv = tn
                        bk = B[3]
                        w0 = bk.war() + [tok_lnfm]
                        tg = None
                        for kc in range(NKC):
                            tg = E.mm(bk.t[0:64, :],
                                      wkva[:, kc * 576 + 512:kc * 576 + 576],
                                      ln_fm.t[:, kc * 512:(kc + 1) * 512],
                                      start=(kc == 0), stop=(kc == NKC - 1),
                                      waits=w0 if kc == 0 else (), inc=(kc == NKC - 1))
                        bk.wr = tg
                        ln_fm.rd.append(tg)
                        ts = tmp_bf[0]
                        tev = E.act(lambda e, t=ts: e.activation(out=t.t[0:64, :],
                                                                 in_=bk.t[0:64, :],
                                                                 func=AF.Copy),
                                    waits=ts.war() + [tg])
                        bk.rd.append(tev)
                        bp = B[4]
                        tp = E.mm(bp.t[0:64, :], perm[0:64, 0:64], ts.t[0:64, :],
                                  start=True, stop=True, waits=bp.war() + [tev], inc=True)
                        bp.wr = tp
                        ts.rd.append(tp)
                        ta = E.dve(lambda e, t=ts, c=cs: e.tensor_tensor(
                            out=a_f32[0:64, :], in0=t.t[0:64, :], in1=c.t[0:64, :],
                            op=OP.mult), waits=[tev, tcos])
                        ts.rd.append(ta)
                        cs.rd.append(ta)
                        tb = E.dve(lambda e, b=bp, s=sn: e.tensor_tensor(
                            out=b_f32[0:64, :], in0=b.t[0:64, :], in1=s.t[0:64, :],
                            op=OP.mult), waits=[tp, tsin])
                        bp.rd.append(tb)
                        sn.rd.append(tb)
                        E.drain("dve")
                        trot = E.dve(lambda e, t=ts: e.tensor_tensor(
                            out=t.t[0:64, :], in0=a_f32[0:64, :], in1=b_f32[0:64, :],
                            op=OP.add), waits=[ta, tb])
                        ts.rd.append(trot)
                        E.drain("dve")
                        bp2 = B[5]
                        tdup = E.mm(bp2.t[:], dup[:], ts.t[0:64, :], start=True, stop=True,
                                    waits=bp2.war() + [trot], inc=True)
                        bp2.wr = tdup
                        ts.rd.append(tdup)
                        tk2 = E.act(lambda e: e.activation(out=krope2[:, tok0:tok0 + 512],
                                                           in_=bp2.t[:], func=AF.Copy),
                                    waits=[tdup])
                        bp2.rd.append(tk2)
                        tskr = E.act(lambda e: e.activation(out=sq_kr.t[:], in_=bp2.t[0:64, :],
                                                            func=AF.Square),
                                     waits=sq_kr.war() + [tdup])
                        bp2.rd.append(tskr)

                        kn_toks = []
                        for h in range(H):
                            bk = B[2 + h % 2]
                            w0 = bk.war() + [tok_ckv, tfold]
                            tg = None
                            for lc in range(4):
                                tg = E.mm(bk.t[:],
                                          wkvb_kn[:, lc * 2048 + h * 128:lc * 2048 + (h + 1) * 128],
                                          ckv_b.t[:, lc * 512:(lc + 1) * 512],
                                          start=(lc == 0), stop=(lc == 3),
                                          waits=w0 if lc == 0 else (), inc=(lc == 3))
                            bk.wr = tg
                            ckv_b.rd.append(tg)
                            tev = E.act(lambda e, b=bk, hh=h: e.activation(
                                out=kn_st.t[:, hh * 512:(hh + 1) * 512], in_=b.t[:],
                                func=AF.Copy), waits=(kn_st.war() if h == 0 else []) + [tg])
                            bk.rd.append(tev)
                            kn_toks.append(tev)
                            sq = sq_bf[h % 2]
                            tsq = E.act(lambda e, b=bk, s=sq: e.activation(
                                out=s.t[:], in_=b.t[:], func=AF.Square),
                                waits=sq.war() + [tg])
                            bk.rd.append(tsq)
                            bn = B[6 + h % 2]
                            wn = bn.war() + [tsq]
                            E.mm(bn.t[0:1, :], ones_c[:], sq.t[:], start=True, stop=False,
                                 waits=wn)
                            tnk = E.mm(bn.t[0:1, :], ones_c[0:64, :], sq_kr.t[:],
                                       start=False, stop=True, waits=[tskr], inc=True)
                            bn.wr = tnk
                            sq.rd.append(tnk)
                            sq_kr.rd.append(tnk)
                            rr = rrow[2 + h % 2]
                            tevn = E.act(lambda e, b=bn, r=rr: e.activation(
                                out=r.t[:], in_=b.t[0:1, :], func=AF.Copy),
                                waits=rr.war() + [tnk])
                            bn.rd.append(tevn)
                            tred = E.dve(lambda e, r=rr, n=nmx: e.reduce_max(
                                out=n.t[0:1, h % 2:h % 2 + 1], in_=r.t[:], axis=AX.X),
                                waits=nmx.war() + [tevn])
                            rr.rd.append(tred)
                            E.drain("dve")
                            tmx = E.dve(lambda e, hh=h, n=nmx: e.tensor_tensor(
                                out=nkmax[0:1, hh:hh + 1], in0=nkmax[0:1, hh:hh + 1],
                                in1=n.t[0:1, hh % 2:hh % 2 + 1], op=OP.max), waits=[tred])
                            nmx.rd.append(tmx)
                            E.drain("dve")
                        kn_st.rd.append(
                            E.dma("scq", kn_scr[:, :, tok0:tok0 + 512],
                                  kn_st.t[:, :].rearrange("p (h t) -> p h t", t=512),
                                  waits=kn_toks))

                        for pt in range(4):
                            vs = v_st[pt % 2]
                            v_toks = []
                            for vg in range(4):
                                bk = B[2 + vg % 2]
                                w0 = bk.war() + [tok_ckv, tfold]
                                tg = None
                                for lc in range(4):
                                    tg = E.mm(bk.t[:],
                                              ckv_b.t[:, lc * 512 + pt * 128:lc * 512 + (pt + 1) * 128],
                                              wkvb_v[:, lc * 2048 + vg * 512:lc * 2048 + (vg + 1) * 512],
                                              start=(lc == 0), stop=(lc == 3),
                                              waits=w0 if lc == 0 else (), inc=(lc == 3))
                                bk.wr = tg
                                ckv_b.rd.append(tg)
                                tev = E.act(lambda e, b=bk, v=vg, s=vs: e.activation(
                                    out=s.t[:, v * 512:(v + 1) * 512],
                                    in_=b.t[:], func=AF.Copy),
                                    waits=(vs.war() if vg == 0 else []) + [tg])
                                bk.rd.append(tev)
                                v_toks.append(tev)
                            vs.rd.append(
                                E.dma("scq", v_scr[tok0 + pt * 128:tok0 + (pt + 1) * 128, :],
                                      vs.t[:], waits=v_toks))
                    E.barrier()

            if "B" in phases:
                if True:
                    ab.reset(); af.reset()
                    q_sb = Bk(ab.alloc(128, NQC * 512))
                    knb = [Bk(ab.alloc(128, 8 * 512)) for k in range(3)]
                    vbuf = [Bk(ab.alloc(128, 2 * 2048)) for k in range(3)]
                    O_sb = Bk(af.alloc(128, H * 512))
                    O_bf = Bk(ab.alloc(128, H * 512))
                    P_sb2 = [Bk(ab.alloc(128, 4 * 512)) for k in range(2)]
                    PT_sb = [Bk(ab.alloc(128, 512)) for k in range(4)]
                    bias_sb = Bk(af.alloc(128, 64))
                    l_col = Bk(af.alloc(128, 64))
                    l_tmp = [Bk(af.alloc(128, 1)) for k in range(8)]
                    l_tmp4 = Bk(af.alloc(128, 4))
                    sq_m = [Bk(ab.alloc(128, 512)) for k in range(2)]
                    mrow = Bk(af.alloc(1, 512))
                    mrow_bf = Bk(ab.alloc(1, 512))
                    lrows = [Bk(af.alloc(1, 512)) for k in range(2)]
                    l_bf = Bk(ab.alloc(128, 64))
                    rrow_b = Bk(af.alloc(1, 512))
                    rrow_bb = Bk(ab.alloc(1, 512))
                    lT = Bk(af.alloc(64, 128))
                    wo_dc = Bk(ab.alloc(128, H * 512))
                    x_jit = Bk(af.alloc(128, 4 * 512))
                    x1_st = [Bk(af.alloc(128, 512)) for k in range(2)]
                    lrow_state = Bk(None)

                    for i in range(NST):
                        tok0 = i * ST
                        tq = E.dma("sp", q_sb.t[:],
                                   q_scr[:, :, tok0:tok0 + 512].rearrange("qc p t -> p qc t"),
                                   waits=q_sb.war())
                        bbs = B[6]
                        wbias = bbs.war()
                        first_bias = True
                        tt = None
                        for hp in range(8):
                            sqr = sq_m[0]
                            trp = E.act(lambda e, c=16 + hp, s=sqr: e.activation(
                                out=s.t[:], in_=q_sb.t[:, c * 512:(c + 1) * 512],
                                func=AF.Square), waits=sqr.war() + [tq])
                            q_sb.rd.append(trp)
                            for hh in range(2):
                                h = 2 * hp + hh
                                sqn = sq_m[1]
                                tn = E.act(lambda e, c=h, s=sqn: e.activation(
                                    out=s.t[:], in_=q_sb.t[:, c * 512:(c + 1) * 512],
                                    func=AF.Square), waits=sqn.war() + [tq])
                                q_sb.rd.append(tn)
                                bq = B[4 + h % 2]
                                wn = bq.war() + [tn]
                                E.mm(bq.t[0:1, :], ones_c[:], sqn.t[:], start=True, stop=False,
                                     waits=wn)
                                rof = hh * 64
                                tnq = E.mm(bq.t[0:1, :], ones_c[rof:rof + 64, :],
                                           sqr.t[rof:rof + 64, :], start=False, stop=True,
                                           waits=[trp], inc=True)
                                bq.wr = tnq
                                sqn.rd.append(tnq)
                                sqr.rd.append(tnq)
                                tm1 = E.dve(lambda e, hh2=h, b=bq: e.tensor_scalar(
                                    out=mrow.t[:], in0=b.t[0:1, :],
                                    scalar1=nkmax[0:1, hh2:hh2 + 1], scalar2=None,
                                    op0=OP.mult), waits=mrow.war() + [tnq])
                                bq.rd.append(tm1)
                                tm2 = E.act(lambda e: e.activation(
                                    out=mrow_bf.t[:], in_=mrow.t[:], func=AF.Sqrt,
                                    bias=0.0, scale=1.0), waits=mrow_bf.war() + [tm1])
                                mrow.rd.append(tm2)
                                for qp in range(4):
                                    tt = E.mm(bbs.t[:, h * 4 + qp:h * 4 + qp + 1],
                                              mrow_bf.t[0:1, qp * 128:(qp + 1) * 128], one1[:],
                                              start=True, stop=True,
                                              waits=(wbias if first_bias else []) + [tm2],
                                              inc=(h == H - 1 and qp == 3))
                                    first_bias = False
                                    mrow_bf.rd.append(tt)
                        bbs.wr = tt
                        tbias = E.act(lambda e: e.activation(out=bias_sb.t[:],
                                                             in_=bbs.t[:, 0:64],
                                                             func=AF.Copy, scale=-SCALE),
                                      waits=bias_sb.war() + [tt])
                        bbs.rd.append(tbias)
                        E.drain("act")
                        tlz = E.dve(lambda e: e.memset(l_col.t[:], 0.0),
                                    waits=l_col.war())
                        E.drain("dve")

                        for j in range(i + 1):
                            diag = (j == i)
                            k0 = j * ST
                            for hf in range(2):
                                ks = knb[(2 * j + hf) % 3]
                                ks.tok = E.dma("sp", ks.t[:],
                                               kn_scr[:, hf * 8:(hf + 1) * 8, k0:k0 + 512],
                                               waits=ks.war())
                                vs = vbuf[(2 * j + hf) % 3]
                                vs.tok = E.dma("sp", vs.t[:],
                                               v_scr[k0 + hf * 256:k0 + (hf + 1) * 256, :]
                                               .rearrange("(pt p) v -> p pt v", p=128),
                                               waits=vs.war())

                            def front(h):
                                ks = knb[(2 * j + h // 8) % 3]
                                kcol = (h % 8) * 512
                                rc = 16 + h // 2
                                rof = (h % 2) * 64
                                P_sb = P_sb2[h % 2]
                                st_ = {"h": h, "P": P_sb, "exp": [], "sb": []}
                                for qp in range(4):
                                    Nk = (qp + 1) * 128 if diag else 512
                                    bs = B[(0, 1, 6, 7)[qp]]
                                    w0 = bs.war() + [tq, ks.tok, tbias]
                                    E.mm(bs.t[:, 0:Nk],
                                         q_sb.t[:, h * 512 + qp * 128:h * 512 + (qp + 1) * 128],
                                         ks.t[:, kcol:kcol + Nk],
                                         start=True, stop=False, waits=w0)
                                    tgs = E.mm(bs.t[:, 0:Nk],
                                               q_sb.t[rof:rof + 64,
                                                      rc * 512 + qp * 128:rc * 512 + (qp + 1) * 128],
                                               krope2[rof:rof + 64, k0:k0 + Nk],
                                               start=False, stop=True, inc=True)
                                    bs.wr = tgs
                                    ks.rd.append(tgs)
                                    q_sb.rd.append(tgs)
                                    lt = l_tmp[(h % 2) * 4 + qp]
                                    if diag:
                                        texp = E.act(lambda e, b=bs, n=Nk, idx=h * 4 + qp, q=qp,
                                                     P=P_sb: e.activation(
                                                         out=P.t[:, q * 512:q * 512 + n],
                                                         in_=b.t[:, 0:n], func=AF.Exp,
                                                         bias=bias_sb.t[:, idx:idx + 1],
                                                         scale=SCALE),
                                                     waits=(P_sb.war() if qp == 0 else []) + [tgs])
                                    else:
                                        texp = E.act(lambda e, b=bs, idx=h * 4 + qp, q=qp, l=lt,
                                                     P=P_sb: e.activation(
                                                         out=P.t[:, q * 512:(q + 1) * 512],
                                                         in_=b.t[:], func=AF.Exp,
                                                         bias=bias_sb.t[:, idx:idx + 1],
                                                         scale=SCALE, accum_out=l.t[:]),
                                                     waits=(P_sb.war() if qp == 0 else []) + lt.war() + [tgs])
                                    bias_sb.rd.append(texp)
                                    P_sb.wr = texp
                                    bs.rd.append(texp)
                                    st_["exp"].append(texp)
                                return st_

                            def tail(st_):
                                h = st_["h"]
                                P_sb = st_["P"]
                                p_toks = st_["exp"]
                                if diag:
                                    for qp in range(4):
                                        c0 = qp * 512 + qp * 128
                                        tmsk = E.dve(lambda e, c=c0, P=P_sb: e.tensor_tensor(
                                            out=P.t[:, c:c + 128],
                                            in0=P.t[:, c:c + 128],
                                            in1=tri[:], op=OP.mult), waits=[p_toks[qp]])
                                        P_sb.wr = tmsk
                                    E.drain("dve")
                                    rts = []
                                    w4 = l_tmp4.war()
                                    for qp in range(4):
                                        Nk = (qp + 1) * 128
                                        trd = E.dve(lambda e, q=qp, n=Nk, P=P_sb: e.reduce_sum(
                                            out=l_tmp4.t[:, q:q + 1],
                                            in_=P.t[:, q * 512:q * 512 + n],
                                            axis=AX.X), waits=w4 if qp == 0 else ())
                                        rts.append(trd)
                                        P_sb.rd.append(trd)
                                    E.drain("dve")
                                    for qp in range(4):
                                        tl = E.dve(lambda e, idx=h * 4 + qp, q=qp: e.tensor_tensor(
                                            out=l_col.t[:, idx:idx + 1],
                                            in0=l_col.t[:, idx:idx + 1],
                                            in1=l_tmp4.t[:, q:q + 1],
                                            op=OP.add), waits=[rts[qp], tlz])
                                        l_tmp4.rd.append(tl)
                                        l_col.wr = tl
                                    p_done = ("dve", E.c["dve"])
                                else:
                                    for qp in range(4):
                                        lt = l_tmp[(h % 2) * 4 + qp]
                                        tl = E.dve(lambda e, idx=h * 4 + qp, l=lt: e.tensor_tensor(
                                            out=l_col.t[:, idx:idx + 1],
                                            in0=l_col.t[:, idx:idx + 1], in1=l.t[:],
                                            op=OP.add), waits=[p_toks[qp], tlz])
                                        lt.rd.append(tl)
                                        l_col.wr = tl
                                    p_done = ("act", E.c["act"])
                                pt_ev = []
                                for ksb in range(4):
                                    bt = B[2 + ksb % 2]
                                    w0 = bt.war() + [p_done]
                                    first = True
                                    tg = None
                                    for qp in (range(ksb, 4) if diag else range(4)):
                                        tg = E.mm(bt.t[:, qp * 128:(qp + 1) * 128],
                                                  P_sb.t[:, qp * 512 + ksb * 128:qp * 512 + (ksb + 1) * 128],
                                                  ident[:], start=True, stop=True,
                                                  waits=w0 if first else (), inc=(qp == 3))
                                        first = False
                                    bt.wr = tg
                                    P_sb.rd.append(tg)
                                    c0 = ksb * 128 if diag else 0
                                    pts = PT_sb[ksb]
                                    tevp = E.act(lambda e, b=bt, c=c0, k=ksb: e.activation(
                                        out=PT_sb[k].t[:, c:512], in_=b.t[:, c:512],
                                        func=AF.Copy), waits=pts.war() + [tg])
                                    bt.rd.append(tevp)
                                    pt_ev.append((ksb, tevp))
                                bo = B[4 + h % 2]
                                w0 = bo.war()
                                tg = None
                                touched = []
                                for idx, (ksb, tevp) in enumerate(pt_ev):
                                    vs = vbuf[(2 * j + ksb // 2) % 3]
                                    c0 = ksb * 128 if diag else 0
                                    tg = E.mm(bo.t[:, c0:512],
                                              vs.t[:, (ksb % 2) * 2048 + h * 128:(ksb % 2) * 2048 + (h + 1) * 128],
                                              PT_sb[ksb].t[:, c0:512],
                                              start=(idx == 0), stop=(idx == len(pt_ev) - 1),
                                              waits=(w0 if idx == 0 else []) + [tevp, vs.tok],
                                              inc=(idx == len(pt_ev) - 1))
                                    touched.append((vs, ksb))
                                for vs, ksb in touched:
                                    vs.rd.append(tg)
                                    PT_sb[ksb].rd.append(tg)
                                bo.wr = tg
                                if j == 0:
                                    to = E.dve(lambda e, hh=h, b=bo: e.tensor_copy(
                                        out=O_sb.t[:, hh * 512:(hh + 1) * 512], in_=b.t[:]),
                                        waits=(O_sb.war() if h == 0 else []) + [tg])
                                else:
                                    to = E.dve(lambda e, hh=h, b=bo: e.tensor_tensor(
                                        out=O_sb.t[:, hh * 512:(hh + 1) * 512],
                                        in0=O_sb.t[:, hh * 512:(hh + 1) * 512],
                                        in1=b.t[:], op=OP.add), waits=[tg])
                                bo.rd.append(to)
                                O_sb.wr = to

                            prev_st = None
                            for h in range(H):
                                st_ = front(h)
                                if prev_st is not None:
                                    tail(prev_st)
                                prev_st = st_
                            tail(prev_st)
                            E.drain("dve")

                        if "o_dbg" in dump and i == 1:
                            O_sb.rd.append(E.dma("scq", o_dbg, O_sb.t[:],
                                                 waits=[("dve", E.c["dve"])]))
                        if "l_dbg" in dump and i == 0:
                            l_col.rd.append(E.dma("scq", l_dbg, l_col.t[:],
                                                  waits=[("dve", E.c["dve"])]))
                        tcb = E.dve(lambda e: e.tensor_copy(out=l_bf.t[:], in_=l_col.t[:]),
                                    waits=l_bf.war())
                        l_col.rd.append(tcb)
                        l_bf.wr = tcb
                        E.drain("dve")
                        blt = B[6]
                        tlt = E.mm(blt.t[0:64, 0:128], l_bf.t[:], ident[:], start=True,
                                   stop=True, waits=blt.war() + [tcb], inc=True)
                        l_bf.rd.append(tlt)
                        blt.wr = tlt
                        tle = E.act(lambda e: e.activation(out=lT.t[:], in_=blt.t[0:64, 0:128],
                                                           func=AF.Copy),
                                    waits=lT.war() + [tlt])
                        blt.rd.append(tle)
                        tlw = E.dma("scq",
                                    lrow_scr.rearrange("a (p c) -> (a p) c", p=64),
                                    lT.t[:], waits=lrow_state.war() + [tle])
                        lT.rd.append(tlw)
                        if "linv_scr" in dump:
                            lT.rd.append(E.dma("scq", linv_scr[i], lT.t[:], waits=[tle]))
                        for h in range(H):
                            lr = lrows[h % 2]
                            tlr = E.dma("sp", lr.t[:],
                                        lrow_scr[:, h * 512:(h + 1) * 512],
                                        waits=lr.war() + [tlw])
                            lrow_state.rd.append(tlr)
                            trc = E.dve(lambda e, l=lr: e.reciprocal(
                                out=rrow_b.t[:], in_=l.t[:]),
                                waits=rrow_b.war() + [tlr])
                            lr.rd.append(trc)
                            E.drain("dve")
                            trcb = E.act(lambda e: e.activation(
                                out=rrow_bb.t[:], in_=rrow_b.t[:], func=AF.Copy),
                                waits=rrow_bb.war() + [trc])
                            rrow_b.rd.append(trcb)
                            bf_ = B[7 if h % 2 == 0 else 3]
                            tfc = E.mm(bf_.t[:], ones_rb[:], rrow_bb.t[:],
                                       start=True, stop=True, waits=bf_.war() + [trcb],
                                       inc=True)
                            bf_.wr = tfc
                            rrow_bb.rd.append(tfc)
                            tob = E.dve(lambda e, hh=h, b=bf_: e.tensor_tensor(
                                out=O_bf.t[:, hh * 512:(hh + 1) * 512],
                                in0=O_sb.t[:, hh * 512:(hh + 1) * 512],
                                in1=b.t[:], op=OP.mult),
                                waits=(O_bf.war() if h == 0 else []) + [tfc, O_sb.wr])
                            bf_.rd.append(tob)
                            O_sb.rd.append(tob)
                            O_bf.wr = tob
                        tok_obf = ("dve", E.c["dve"])
                        if "attn_scr" in dump:
                            O_bf.rd.append(
                                E.dma("scq", attn_scr[:, :, tok0:tok0 + 512],
                                      O_bf.t[:, :].rearrange("p (h t) -> p h t", t=512),
                                      waits=[tok_obf]))

                        for dc in range(4):
                            two = E.dma("sp", wo_dc.t[:, :].rearrange("p (h d) -> p h d", d=512),
                                        woB.rearrange("(h p) d -> p h d", p=128)
                                        [:, :, dc * 512:(dc + 1) * 512], waits=wo_dc.war())
                            txj = E.dma("sp", x_jit.t[:, :].rearrange("p (pt d) -> p pt d", d=512),
                                        x_in[tok0:tok0 + 512, dc * 512:(dc + 1) * 512]
                                        .rearrange("(pt p) d -> p pt d", p=128),
                                        waits=x_jit.war())
                            for qp in range(4):
                                bop = B[qp % 2]
                                w0 = bop.war() + [two, tok_obf]
                                tg = None
                                for h in range(H):
                                    tg = E.mm(bop.t[:],
                                              O_bf.t[:, h * 512 + qp * 128:h * 512 + (qp + 1) * 128],
                                              wo_dc.t[:, h * 512:(h + 1) * 512],
                                              start=(h == 0), stop=(h == H - 1),
                                              waits=w0 if h == 0 else (), inc=(h == H - 1))
                                bop.wr = tg
                                wo_dc.rd.append(tg)
                                O_bf.rd.append(tg)
                                xs1 = x1_st[qp % 2]
                                tadd = E.dve(lambda e, b=bop, q=qp: e.tensor_tensor(
                                    out=xs1.t[:], in0=b.t[:],
                                    in1=x_jit.t[:, q * 512:(q + 1) * 512], op=OP.add),
                                    waits=xs1.war() + [tg, txj])
                                bop.rd.append(tadd)
                                x_jit.rd.append(tadd)
                                xs1.rd.append(
                                    E.dma("scq",
                                          x1_scr[tok0 + qp * 128:tok0 + (qp + 1) * 128,
                                                 dc * 512:(dc + 1) * 512],
                                          xs1.t[:], waits=[tadd]))
                    E.barrier()

            if "C" in phases:
                if True:
                    ab.reset(); af.reset()
                    acc = Bk(af.alloc(128, 8 * D))
                    ln2_fm = Bk(ab.alloc(128, NKC * TP))
                    ln2_tm = [Bk(ab.alloc(128, D)) for k in range(2)]
                    h_buf = Bk(ab.alloc(128, 16 * TP))
                    wg_sb = [Bk(ab.alloc(128, NKC * 128)) for k in range(2)]
                    wu_sb = [Bk(ab.alloc(128, NKC * 128)) for k in range(2)]
                    wd_sb = [Bk(ab.alloc(128, 16 * 512)) for k in range(2)]
                    gs_sb = [Bk(ab.alloc(128, 512)) for k in range(2)]
                    lnst2 = af.alloc(128, 16)
                    xsq2 = Bk(af.alloc(128, D))

                    prev = {}
                    for f in range(NTP):
                        t0 = f * TP
                        acc_t = []
                        war = acc.war()
                        for pt in range(8):
                            acc_t.append(E.dma("sp", acc.t[:, pt * D:(pt + 1) * D],
                                               x1_scr[t0 + pt * 128:t0 + (pt + 1) * 128, :],
                                               waits=war if pt == 0 else ()))
                        for pt in range(8):
                            xa = acc.t[:, pt * D:(pt + 1) * D]
                            t1 = E.dve(lambda e, x=xa: e.reduce_sum(out=lnst2[:, 0:1], in_=x,
                                                                    axis=AX.X),
                                       waits=[acc_t[pt], prev.get("c4")])
                            t2 = E.act(lambda e, x=xa: e.activation(out=xsq2.t[:], in_=x,
                                                                    func=AF.Square),
                                       waits=xsq2.war() + [acc_t[pt]])
                            t3 = E.dve(lambda e: e.reduce_sum(out=lnst2[:, 1:2], in_=xsq2.t[:],
                                                              axis=AX.X), waits=[t2])
                            xsq2.rd.append(t3)
                            E.drain("dve")
                            t4 = E.dve(lambda e: e.tensor_scalar(
                                out=lnst2[:, 2:3], in0=lnst2[:, 0:1], scalar1=-1.0 / D,
                                scalar2=None, op0=OP.mult), waits=[prev.get("c5")])
                            prev["c4"] = t4
                            E.drain("dve")
                            t5 = E.act(lambda e: e.activation(out=lnst2[:, 3:4], in_=lnst2[:, 2:3],
                                                              func=AF.Square),
                                       waits=[t4, prev.get("c6")])
                            prev["c5"] = t5
                            t6 = E.dve(lambda e: e.scalar_tensor_tensor(
                                out=lnst2[:, 4:5], in0=lnst2[:, 1:2], scalar=1.0 / D,
                                in1=lnst2[:, 3:4], op0=OP.mult, op1=OP.subtract),
                                waits=[t3, t5])
                            prev["c6"] = t6
                            t7 = E.act(lambda e: e.activation(out=lnst2[:, 5:6], in_=lnst2[:, 4:5],
                                                              func=AF.Sqrt, bias=LN_EPS,
                                                              scale=1.0),
                                       waits=[t6, prev.get("c8")])
                            t8 = E.dve(lambda e: e.reciprocal(out=lnst2[:, 6:7],
                                                              in_=lnst2[:, 5:6]), waits=[t7])
                            prev["c8"] = t8
                            E.drain("dve")
                            ls = ln2_tm[pt % 2]
                            t9 = E.dve(lambda e, l=ls, x=xa: e.tensor_scalar(
                                out=l.t[:], in0=x, scalar1=lnst2[:, 2:3],
                                scalar2=lnst2[:, 6:7], op0=OP.add, op1=OP.mult),
                                waits=ls.war() + [t8])
                            acc.rd.append(t9)
                            E.drain("dve")
                            for kcg in range(4):
                                bk = B[kcg % 2]
                                w0 = bk.war() + ln2_fm.war() + [t9]
                                tg = None
                                for jj in range(4):
                                    kc = kcg * 4 + jj
                                    tg = E.mm(bk.t[:, jj * 128:(jj + 1) * 128],
                                              ls.t[:, kc * 128:(kc + 1) * 128], ident[:],
                                              start=True, stop=True,
                                              waits=w0 if jj == 0 else (), inc=(jj == 3))
                                bk.wr = tg
                                ls.rd.append(tg)
                                out_v = ln2_fm.t[:, :].rearrange("p (kc t) -> p kc t", t=TP)[
                                    :, kcg * 4:(kcg + 1) * 4, pt * 128:(pt + 1) * 128]
                                te = E.act(lambda e, b=bk, o=out_v: e.activation(
                                    out=o, in_=b.t[:, :].rearrange("p (j m) -> p j m", m=128),
                                    func=AF.Copy), waits=[tg])
                                bk.rd.append(te)
                                ln2_fm.wr = te
                        tok_ln2 = ("act", E.c["act"])
                        if "ln2_dbg" in dump:
                            ln2_fm.rd.append(E.dma("scq", ln2_dbg[f], ln2_fm.t[:],
                                                   waits=[tok_ln2]))

                        for w in range(4):
                            hwar = h_buf.war()
                            for ii in range(16):
                                ic = w * 16 + ii
                                wgs = wg_sb[ii % 2]
                                twg = E.dma("sp", wgs.t[:],
                                            wguB[:, :, ic * 128:(ic + 1) * 128]
                                            .rearrange("kc p m -> p kc m"), waits=wgs.war())
                                wus = wu_sb[ii % 2]
                                twu = E.dma("sp", wus.t[:],
                                            wguB[:, :, INTER + ic * 128:INTER + (ic + 1) * 128]
                                            .rearrange("kc p m -> p kc m"), waits=wus.war())
                                for th in range(2):
                                    bg = B[2 + th]
                                    w0 = bg.war() + [twg, tok_ln2]
                                    tg = None
                                    for kc in range(NKC):
                                        tg = E.mm(bg.t[:], wgs.t[:, kc * 128:(kc + 1) * 128],
                                                  ln2_fm.t[:, kc * TP + th * 512:kc * TP + (th + 1) * 512],
                                                  start=(kc == 0), stop=(kc == NKC - 1),
                                                  waits=w0 if kc == 0 else (),
                                                  inc=(kc == NKC - 1))
                                    bg.wr = tg
                                    wgs.rd.append(tg)
                                    ln2_fm.rd.append(tg)
                                    bu = B[4 + th]
                                    w0 = bu.war() + [twu, tok_ln2]
                                    tu = None
                                    for kc in range(NKC):
                                        tu = E.mm(bu.t[:], wus.t[:, kc * 128:(kc + 1) * 128],
                                                  ln2_fm.t[:, kc * TP + th * 512:kc * TP + (th + 1) * 512],
                                                  start=(kc == 0), stop=(kc == NKC - 1),
                                                  waits=w0 if kc == 0 else (),
                                                  inc=(kc == NKC - 1))
                                    bu.wr = tu
                                    wus.rd.append(tu)
                                    ln2_fm.rd.append(tu)
                                    gs = gs_sb[th]
                                    tsi = E.act(lambda e, b=bg, g=gs: e.activation(
                                        out=g.t[:], in_=b.t[:], func=AF.Silu),
                                        waits=gs.war() + [tg])
                                    bg.rd.append(tsi)
                                    tmu = E.dve(lambda e, g=gs, b=bu, i2=ii, t2=th:
                                                e.tensor_tensor(
                                                    out=h_buf.t[:, i2 * TP + t2 * 512:i2 * TP + (t2 + 1) * 512],
                                                    in0=g.t[:], in1=b.t[:], op=OP.mult),
                                                waits=hwar + [tsi, tu])
                                    gs.rd.append(tmu)
                                    bu.rd.append(tmu)
                                    h_buf.wr = tmu
                            tok_h = ("dve", E.c["dve"])
                            for dc in range(4):
                                wds = wd_sb[dc % 2]
                                twd = E.dma("sp", wds.t[:],
                                            wdnB[w * 2048:(w + 1) * 2048, dc * 512:(dc + 1) * 512]
                                            .rearrange("(ii p) d -> p ii d", p=128),
                                            waits=wds.war())
                                for ts2 in range(8):
                                    bd = B[6 + ts2 % 2]
                                    w0 = bd.war() + [twd, tok_h]
                                    tg = None
                                    for ii in range(16):
                                        tg = E.mm(bd.t[:],
                                                  h_buf.t[:, ii * TP + ts2 * 128:ii * TP + (ts2 + 1) * 128],
                                                  wds.t[:, ii * 512:(ii + 1) * 512],
                                                  start=(ii == 0), stop=(ii == 15),
                                                  waits=w0 if ii == 0 else (), inc=(ii == 15))
                                    bd.wr = tg
                                    wds.rd.append(tg)
                                    h_buf.rd.append(tg)
                                    tac = E.dve(lambda e, b=bd, t2=ts2, d=dc: e.tensor_tensor(
                                        out=acc.t[:, t2 * D + d * 512:t2 * D + (d + 1) * 512],
                                        in0=acc.t[:, t2 * D + d * 512:t2 * D + (d + 1) * 512],
                                        in1=b.t[:], op=OP.add), waits=[tg])
                                    bd.rd.append(tac)
                                    acc.rd.append(tac)
                            E.drain("dve")
                        tok_acc = ("dve", E.c["dve"])
                        for pt in range(8):
                            acc.rd.append(
                                E.dma("scq", y_out[t0 + pt * 128:t0 + (pt + 1) * 128, :],
                                      acc.t[:, pt * D:(pt + 1) * D], waits=[tok_acc]))
                    E.barrier()

        @blk.sync
        def _(eng):
            schedule(Emitter(nc, "sp", eng, sems))

        @blk.gpsimd
        def _(eng):
            schedule(Emitter(nc, "pool", eng, sems))

        @blk.vector
        def _(eng):
            schedule(Emitter(nc, "dve", eng, sems))

        @blk.scalar
        def _(eng):
            schedule(Emitter(nc, "act", eng, sems))

        @blk.tensor
        def _(eng):
            schedule(Emitter(nc, "te", eng, sems))

    return nc



def _host_constants():
    c = {}
    c["ident_bf"] = np.eye(128, dtype=np.float32).astype(BFDT)
    permm = np.zeros((128, 128), dtype=np.float32)
    for r in range(128):
        s = r + 32 if (r // 32) % 2 == 0 else r - 32
        permm[s, r] = 1.0
    c["perm_bf"] = permm.astype(BFDT)
    dupm = np.zeros((64, 128), dtype=np.float32)
    for m in range(128):
        dupm[m % 64, m] = 1.0
    c["dup_bf"] = dupm.astype(BFDT)
    q_i = np.arange(128)[:, None]
    k_i = np.arange(128)[None, :]
    c["tri_bf"] = (k_i <= q_i).astype(np.float32).astype(BFDT)
    c["i128_f32"] = np.eye(128, dtype=np.float32)
    j = np.arange(0, ROPE, 2, dtype=np.float32) / np.float32(ROPE)
    freqs = (1.0 / (np.float32(10000.0) ** j)).astype(np.float32)
    t = np.arange(S, dtype=np.float32)[:, None] * freqs[None, :]
    cos = np.cos(t).astype(np.float32).T
    sin = np.sin(t).astype(np.float32).T
    c["cosq"] = np.ascontiguousarray(np.concatenate([cos, cos, cos, cos], 0))
    c["sinq"] = np.ascontiguousarray(np.concatenate([-sin, sin, -sin, sin], 0))
    return c


def _perm_inputs(Wq, Wkva, Wkvb, kvw, Wgu, Wdown):
    cols = []
    for h in range(H):
        cols.extend(range(h * 192, h * 192 + 128))
    for h in range(H):
        cols.extend(range(h * 192 + 128, h * 192 + 192, 2))
        cols.extend(range(h * 192 + 129, h * 192 + 192, 2))
    wq_p = np.ascontiguousarray(Wq[:, cols])
    cols = list(range(512)) + list(range(512, 576, 2)) + list(range(513, 576, 2))
    wkva_p = np.ascontiguousarray(Wkva[:, cols])
    cols = []
    for h in range(H):
        cols.extend(range(h * 256, h * 256 + 128))
    for h in range(H):
        cols.extend(range(h * 256 + 128, h * 256 + 256))
    wkvb_p = np.ascontiguousarray(Wkvb[:, cols])
    kvw_t = np.ascontiguousarray(kvw.reshape(4, 128).T)
    wguT = np.ascontiguousarray(Wgu.T)
    wdownT = np.ascontiguousarray(Wdown.T)
    return wq_p, wkva_p, wkvb_p, kvw_t, wguT, wdownT


def make_inputs(x_b, Wq, Wkva, kvw, Wkvb, Wo, Wgu, Wdown):
    wq_p, wkva_p, wkvb_p, kvw_t, wguT, wdownT = _perm_inputs(
        np.asarray(Wq, np.float32), np.asarray(Wkva, np.float32),
        np.asarray(Wkvb, np.float32), np.asarray(kvw, np.float32),
        np.asarray(Wgu, np.float32), np.asarray(Wdown, np.float32))
    base = {"wq_p": wq_p, "wkva_p": wkva_p, "wkvb_p": wkvb_p, "kvw": kvw_t,
            "wo": np.ascontiguousarray(np.asarray(Wo, np.float32)),
            "wguT": wguT, "wdownT": wdownT, **_host_constants()}
    base["x"] = np.ascontiguousarray(np.asarray(x_b, np.float32))
    return base


def kernel(L0_attn_x, L0_attn_Wq, L0_attn_Wkv_a, L0_attn_kv_norm_w, L0_attn_Wkv_b,
           L0_attn_Wo, L0_ffn_w_gate_up, L0_ffn_w_down):
    x = np.asarray(L0_attn_x, dtype=np.float32)
    base = make_inputs(x[0], L0_attn_Wq, L0_attn_Wkv_a, L0_attn_kv_norm_w,
                       L0_attn_Wkv_b, L0_attn_Wo, L0_ffn_w_gate_up, L0_ffn_w_down)
    nc = build_program()
    in_maps = []
    for b in range(8):
        m = dict(base)
        m["x"] = np.ascontiguousarray(x[b])
        in_maps.append(m)
    res = run_bass_kernel_spmd(nc, in_maps, core_ids=list(range(8)))
    return np.stack([np.asarray(r["y"], np.float32) for r in res.results], axis=0)
